# version 32
# baseline (speedup 1.0000x reference)
import contextlib
import numpy as np
import ml_dtypes
import concourse.bass as bass
import concourse.mybir as mybir
from concourse.bass_utils import run_bass_kernel_spmd

F32 = mybir.dt.float32
BF16 = mybir.dt.bfloat16
FP16 = mybir.dt.float16
AF = mybir.ActivationFunctionType
ALU = mybir.AluOpType
AX = mybir.AxisListType

NB = 2
T = 2048
D = 1024
H = 16
NE = 32
EF = 512
NCORES = 8
NT = T // 128
PI = [0, 0, 0, 1, 1, 2]
PJ = [1, 2, 3, 2, 3, 3]
SGN = [1.0, -1.0, 1.0, 1.0, -1.0, 1.0]
CSHIFT = 40.0
SIMSAFE = True
DMASERIAL = True


class _Op:
    __slots__ = ("eng", "fn", "deps", "dma", "sig", "waits", "note")

    def __init__(self, eng, fn, deps, dma, note):
        self.eng, self.fn, self.deps, self.dma = eng, fn, deps, dma
        self.sig = None
        self.waits = ()
        self.note = note


class Trk:
    __slots__ = ("w", "rs")

    def __init__(self, init=None):
        self.w = init
        self.rs = []


class Sched:
    ENGS = ("pe", "dve", "act", "pool", "sync")

    def __init__(self):
        self.ops = {e: [] for e in self.ENGS}
        self.last = {e: None for e in self.ENGS}

    def add(self, eng, fn, reads=(), writes=(), dma=False, note=""):
        deps = []
        for t in reads:
            if t.w is not None:
                deps.append(t.w)
        for t in writes:
            if t.w is not None:
                deps.append(t.w)
            deps.extend(t.rs)
        op = _Op(eng, fn, deps, dma, note)
        self.ops[eng].append(op)
        self.last[eng] = op
        for t in reads:
            t.rs.append(op)
        for t in writes:
            t.w = op
            t.rs = []
        return op

    def barrier(self, note="bar"):
        deps = [o for o in self.last.values() if o is not None]
        op = _Op("pe", lambda e: e.nop(), deps, False, note)
        self.ops["pe"].append(op)
        self.last["pe"] = op
        return op

    def finalize(self):
        for eng in self.ENGS:
            for op in self.ops[eng]:
                for d in op.deps:
                    if d.dma or d.eng != op.eng or (SIMSAFE and d.eng != "pe"):
                        if d.sig is None:
                            d.sig = "need"
        for eng in self.ENGS:
            cc = dc = 0
            for op in self.ops[eng]:
                if op.dma:
                    dc += 16
                    op.sig = ("d", eng, dc)
                elif op.sig == "need":
                    cc += 1
                    op.sig = ("c", eng, cc)
                else:
                    op.sig = None
        for eng in self.ENGS:
            marks = {}
            for op in self.ops[eng]:
                ws = {}
                for d in op.deps:
                    if d.sig is None:
                        continue
                    if (not d.dma) and d.eng == eng and (
                            eng == "pe" or not SIMSAFE):
                        continue
                    kind, peng, val = d.sig
                    key = (kind, peng)
                    if val > marks.get(key, 0) and val > ws.get(key, 0):
                        ws[key] = val
                if DMASERIAL and op.dma and op.sig[2] > 16:
                    key = ("d", eng)
                    v = op.sig[2] - 16
                    if v > marks.get(key, 0) and v > ws.get(key, 0):
                        ws[key] = v
                for k, v in ws.items():
                    marks[k] = v
                op.waits = tuple(ws.items())

    def emit(self, nc, stack):
        self.finalize()
        sems = {}
        for eng in self.ENGS:
            if any((not o.dma) and o.sig for o in self.ops[eng]):
                sems[("c", eng)] = stack.enter_context(nc.semaphore(f"c_{eng}"))
            if any(o.dma for o in self.ops[eng]):
                sems[("d", eng)] = stack.enter_context(nc.semaphore(f"d_{eng}"))
        block = stack.enter_context(nc.Block())

        def body(eng):
            def run(engine):
                for op in self.ops[eng]:
                    for (key, val) in op.waits:
                        engine.wait_ge(sems[key], val)
                    inst = op.fn(engine)
                    if op.sig is not None:
                        kind, peng, val = op.sig
                        inst.then_inc(sems[(kind, peng)],
                                      16 if kind == "d" else 1)
                if eng == "sync":
                    last = 0
                    for op in self.ops[eng]:
                        if op.dma:
                            last = op.sig[2]
                    if last:
                        engine.wait_ge(sems[("d", "sync")], last)
            return run

        block.tensor(body("pe"))
        block.vector(body("dve"))
        block.scalar(body("act"))
        block.gpsimd(body("pool"))
        block.sync(body("sync"))


def host_consts():
    i_ = np.arange(128)[:, None]
    j_ = np.arange(512)[None, :]
    diffc = np.zeros((128, 2048), np.float16)
    mc = np.zeros((128, 2048), ml_dtypes.bfloat16)
    m2c = np.zeros((128, 2048), ml_dtypes.bfloat16)
    for c in range(4):
        d = (j_ - i_ - 128 * c).astype(np.float32)
        diffc[:, 512 * c:512 * (c + 1)] = d.astype(np.float16)
        mc[:, 512 * c:512 * (c + 1)] = (d > 0).astype(ml_dtypes.bfloat16)
        m2c[:, 512 * c:512 * (c + 1)] = (d >= 0).astype(ml_dtypes.bfloat16)
    eqpat = np.tile((np.arange(2048) % 512).astype(np.float16)[None, :], (128, 1))
    kneg = np.tile((-np.arange(1536)).astype(np.float16)[None, :], (128, 1))
    sel32 = np.zeros((32, 32 * 128), ml_dtypes.bfloat16)
    for e in range(NE):
        sel32[e, 128 * e:128 * (e + 1)] = 1.0
    return {
        "identb": np.eye(128, dtype=np.float32).astype(ml_dtypes.bfloat16),
        "diffc": diffc, "mc": mc, "m2c": m2c,
        "eqpat": eqpat, "kneg": kneg, "sel32": sel32,
    }


NPDT = {np.dtype(np.float32): F32, np.dtype(np.float16): FP16,
        np.dtype(ml_dtypes.bfloat16): BF16}


class MK:
    def __init__(self, stage="full", dbg=()):
        self.stage = stage
        self.dbg = list(dbg)
        self.nc = bass.Bass("TRN2", target_bir_lowering=False)
        self.S = Sched()
        self.ex = contextlib.ExitStack()
        self.dbg_out = {}
        self.bar = None

    def trk(self):
        return Trk(self.bar)

    def sb(self, stack, name, shape, dt):
        t = stack.enter_context(self.nc.sbuf_tensor(name, shape, dt))
        return t, self.trk()

    def pe(self, fn, reads=(), writes=(), note=""):
        return self.S.add("pe", fn, reads, writes, note=note)

    def dve(self, fn, reads=(), writes=(), note=""):
        return self.S.add("dve", fn, reads, writes, note=note)

    def act(self, fn, reads=(), writes=(), note=""):
        return self.S.add("act", fn, reads, writes, note=note)

    def pool(self, fn, reads=(), writes=(), note=""):
        return self.S.add("pool", fn, reads, writes, note=note)

    def dma(self, queue, out, in_, reads=(), writes=(), note=""):
        return self.S.add(queue,
                          lambda e, o=out, i=in_: e.dma_start(out=o, in_=i),
                          reads, writes, dma=True, note=note)

    def mm(self, out, lhsT, rhs, start, stop, reads=(), writes=(), note="",
           tp=None):
        def f(e, o=out, l=lhsT, r=rhs, st=start, sp=stop, t=tp):
            return e.matmul(o, l, r, start=st, stop=sp, tile_position=t)
        return self.pe(f, reads, writes, note=note)

    def tr(self, out, in_, ident, reads=(), writes=(), note=""):
        def f(e, o=out, i=in_, d=ident):
            return e.transpose(o, i, d)
        return self.pe(f, reads, writes, note=note)

    def fill(self, reads=(), bank=0, n=512):
        self.mm(self.P[bank][0:1, 0:n], self.onesb[0:1, 0:1],
                self.onesb[0:1, 0:n], True, True,
                reads=(self.t_onesb,) + tuple(reads),
                writes=(self.tP[bank],))

    def dbg_dump(self, name, b, ap, trk):
        if (name, b) in self.dbg:
            self.dbg.remove((name, b))
            key = f"{name}_b{b}"
            d = self.nc.dram_tensor("dbg_" + key, list(ap.shape),
                                    ap.dtype, kind="ExternalOutput").ap()
            self.dma("sync", d, ap, reads=(trk,), note="dbg" + key)
            self.dbg_out[key] = True

    def build(self):
        nc = self.nc
        ex = self.ex
        self.din = {}
        specs = [("L0_attn_x", [NB, T, D]), ("L0_attn_w_qkv", [3 * D, D]),
                 ("L0_attn_b_qkv", [3 * D]), ("L0_attn_w1_write", [64, D]),
                 ("L0_attn_w2_write", [64, D]), ("L0_attn_w1_read", [64, D]),
                 ("L0_attn_w2_read", [64, D]), ("L0_attn_w_out", [D, D]),
                 ("L0_attn_b_out", [D]), ("L0_attn_decay_logits", [H]),
                 ("L0_attn_bias_scale", [H]), ("L0_ffn_expert_sel", [NE, D]),
                 ("L0_ffn_keys", [NE, D, EF]), ("L0_ffn_values", [NE, EF, D])]
        for nm, shp in specs:
            self.din[nm] = nc.dram_tensor(nm, shp, F32, kind="ExternalInput").ap()
        for nm, arr in host_consts().items():
            self.din["c_" + nm] = nc.dram_tensor(
                "c_" + nm, list(arr.shape), NPDT[arr.dtype],
                kind="ExternalInput").ap()
        self.dout = nc.dram_tensor("out", [NB, T, D], F32,
                                   kind="ExternalOutput").ap()
        self.dscr = nc.dram_tensor("xprime", [NB, T, D], F32,
                                   kind="Internal").ap()
        self.t_dscr = Trk()

        sb = lambda n, s, d: self.sb(ex, n, s, d)
        self.identb, self.t_identb = sb("identb", [128, 128], BF16)
        self.ones, self.t_ones = sb("ones", [128, 512], F32)
        self.onesb, self.t_onesb = sb("onesb", [128, 512], BF16)
        self.borow, self.t_borow = sb("borow", [1, D], BF16)
        self.drow, self.t_drow = sb("drow", [1, H], F32)
        self.ldrow, self.t_ldrow = sb("ldrow", [1, H], F32)
        self.bsrow, self.t_bsrow = sb("bsrow", [1, H], F32)

        self.P2 = [ex.enter_context(nc.psum_tensor(f"pp{i}", [128, 1024], F32))
                   for i in range(4)]
        self.P = [self.P2[i // 2][:, 512 * (i % 2):512 * (i % 2 + 1)]
                  for i in range(8)]
        self.TBk = self.P2[3][:].bitcast(BF16)
        self.tP = [self.trk() for _ in range(8)]
        self.tTB = self.tP[6]

        self.load_consts()
        for b in range(NB):
            self.batch(b)
            if self.stage in ("lines", "qkv"):
                break
        self.S.emit(nc, ex)
        self.ex.close()
        return nc

    def load_consts(self):
        d = self.dma
        d("sync", self.identb[:], self.din["c_identb"], writes=(self.t_identb,))
        r1 = lambda ap: ap.rearrange("(a n) -> a n", a=1)
        d("pool", self.borow[:], r1(self.din["L0_attn_b_out"]), writes=(self.t_borow,))
        d("sync", self.drow[:], r1(self.din["L0_attn_decay_logits"]),
          writes=(self.t_drow,))
        d("sync", self.bsrow[:], r1(self.din["L0_attn_bias_scale"]),
          writes=(self.t_bsrow,))
        self.dve(lambda e: e.memset(self.ones[:], 1.0), writes=(self.t_ones,))
        self.dve(lambda e: e.tensor_copy(self.onesb[:], self.ones[:]),
                 reads=(self.t_ones,), writes=(self.t_onesb,))
        self.act(lambda e: e.activation(out=self.drow[:], in_=self.drow[:],
                 func=AF.Sigmoid), reads=(self.t_drow,), writes=(self.t_drow,))
        self.act(lambda e: e.activation(out=self.ldrow[:], in_=self.drow[:],
                 func=AF.Ln), reads=(self.t_drow,), writes=(self.t_ldrow,))

    def batch(self, b):
        ax = contextlib.ExitStack()
        self.avt, self.t_avt = self.sb(ax, f"avt{b}", [128, 8 * 2048], BF16)
        self.attn(b, ax)
        if self.stage == "full":
            self.wout(b)
        ax.close()
        self.bar = self.S.barrier(f"bar_attn{b}")
        if self.stage == "full":
            self.ffn(b)
        self.bar = self.S.barrier(f"bar_b{b}")

    def attn(self, b, ax):
        sb = lambda n, s, d: self.sb(ax, f"{n}{b}", s, d)
        self.diffc, self.t_diffc = sb("diffc", [128, 2048], FP16)
        self.mc, self.t_mc = sb("mc", [128, 2048], BF16)
        self.m2c, self.t_m2c = sb("m2c", [128, 2048], BF16)
        self.eqpat, self.t_eqpat = sb("eqpat", [128, 2048], FP16)
        self.kneg, self.t_kneg = sb("kneg", [128, 1536], FP16)
        self.brow, self.t_brow = sb("brow", [1, 3 * D], BF16)
        for nm, t_, trk in (("c_diffc", self.diffc, self.t_diffc),
                            ("c_mc", self.mc, self.t_mc),
                            ("c_m2c", self.m2c, self.t_m2c),
                            ("c_eqpat", self.eqpat, self.t_eqpat),
                            ("c_kneg", self.kneg, self.t_kneg)):
            self.dma("sync", t_[:], self.din[nm], writes=(trk,))
        self.dma("pool", self.brow[:],
                 self.din["L0_attn_b_qkv"].rearrange("(a n) -> a n", a=1),
                 writes=(self.t_brow,))
        S, nc = self.S, self.nc
        pe, dve, act, dma, mm, tr = (self.pe, self.dve, self.act, self.dma,
                                     self.mm, self.tr)
        P, tP, TBk, tTB = self.P, self.tP, self.TBk, self.tTB

        xnT, t_xnT = sb("xnT", [128, 8 * 2052], BF16)
        wqT, _ = sb("wqT", [128, 2 * 8 * 384], BF16)
        t_wq = [self.trk() for _ in range(16)]
        w4T, t_w4 = sb("w4T", [128, 8 * 256], BF16)
        linw, t_lw = sb("linw", [128, NT * 96], BF16)
        linr, t_lr = sb("linr", [128, NT * 96], BF16)
        lst, t_lst = sb("lst", [128, 8 * 96], F32)
        l2s, t_l2s = sb("l2s", [128, 8 * 96], F32)
        nst, t_nst = sb("nst", [128, 8 * 16], F32)
        tm1, t_tm1 = sb("tm1", [128, 8 * 16], F32)
        qt_, t_qt = sb("qt", [128, 2048], BF16)
        kt_, t_kt = sb("kt", [128, 2048], BF16)
        vp_, t_vp = sb("vp", [128, NT * 132], BF16)
        rtg, t_rtg = sb("rtg", [128, 2048], BF16)
        jwb, t_jwb = sb("jwb", [128, 2048], BF16)
        rpp, t_rpp = sb("rpp", [128, 2048], BF16)
        jpp, t_jpp = sb("jpp", [128, 3 * 1536], BF16)
        ebuf, t_eb = sb("ebuf", [128, 1536], BF16)
        wdg, t_wdg = sb("wdg", [128, 8 * 512], BF16)
        trs, t_trs = sb("trs", [32, 2048], BF16)
        negc, t_negc = sb("negc", [128, 2 * 512], BF16)
        les, t_les = sb("les", [128, 32], F32)
        nc4, t_nc4 = sb("nc4", [128, 8], BF16)
        ptb, t_ptb = sb("ptb", [128, 3 * 1024], BF16)
        dum, t_dum = sb("dum", [128, 512], BF16)
        tdg, t_tdg = sb("tdg", [128, 2 * 512], BF16)
        avu, t_avu = sb("avu", [128, 1024], BF16)
        lin, t_lin = sb("lin", [128, 1024], BF16)
        xt_, t_xt = sb("xt", [128, 2 * 1024], F32)
        extb, t_ext = xt_, t_xt
        xnstg, t_xns = sb("xnstg", [128, 1024], BF16)
        wld, t_wld = sb("wld", [128, 4 * 1024], BF16)
        t_wldS = [self.trk() for _ in range(4)]
        self._wld_pref = False
        ldg, t_ldg = sb("ldg", [128, 1], F32)
        bsg, t_bsg = sb("bsg", [128, 1], F32)
        lqb, t_lqb = sb("lqb", [128, 1], F32)
        st1, t_st1 = sb("st1", [128, 16], F32)
        ldbc, t_ldbc = sb("ldbc", [128, H], F32)
        dx = self.din["L0_attn_x"]

        mm(P[6][:, 0:H], self.ones[0:1, 0:128], self.ldrow[:], True, True,
           reads=(self.t_ones, self.t_ldrow), writes=(tP[6],))
        dve(lambda e: e.tensor_copy(ldbc[:], P[6][:, 0:H]),
            reads=(tP[6],), writes=(t_ldbc,))

        for c in range(8):
            dve(lambda e, c=c: e.memset(xnT[:, 2052 * c:2052 * c + 1], 0.0),
                writes=(t_xnT,))
            dve(lambda e, c=c: e.memset(xnT[:, 2052 * c + 2049:2052 * (c + 1)],
                0.0), writes=(t_xnT,))
        for tt in range(NT):
            xsl = xt_[:, 1024 * (tt % 2):1024 * (tt % 2 + 1)]
            dma("sync", xsl, dx[b, 128 * tt:128 * (tt + 1), :], writes=(t_xt,))
            dve(lambda e, xs=xsl: e.tensor_reduce(
                out=st1[:, 0:1], in_=xs, axis=AX.X, op=ALU.add),
                reads=(t_xt,), writes=(t_st1,))
            dve(lambda e: e.tensor_scalar(out=st1[:, 0:1], in0=st1[:, 0:1],
                scalar1=1.0 / D, scalar2=None, op0=ALU.mult),
                reads=(t_st1,), writes=(t_st1,))
            act(lambda e, xs=xsl: e.activation(
                out=xnstg[:], in_=xs, func=AF.Square, accum_out=st1[:, 1:2]),
                reads=(t_xt,), writes=(t_xns, t_st1))
            dve(lambda e: e.tensor_scalar(out=st1[:, 1:2], in0=st1[:, 1:2],
                scalar1=1.0 / D, scalar2=None, op0=ALU.mult),
                reads=(t_st1,), writes=(t_st1,))
            dve(lambda e: e.tensor_tensor(out=st1[:, 2:3], in0=st1[:, 0:1],
                in1=st1[:, 0:1], op=ALU.mult), reads=(t_st1,), writes=(t_st1,))
            dve(lambda e: e.tensor_tensor(out=st1[:, 1:2], in0=st1[:, 1:2],
                in1=st1[:, 2:3], op=ALU.subtract),
                reads=(t_st1,), writes=(t_st1,))
            dve(lambda e: e.tensor_scalar(out=st1[:, 1:2], in0=st1[:, 1:2],
                scalar1=1e-5, scalar2=None, op0=ALU.add),
                reads=(t_st1,), writes=(t_st1,))
            act(lambda e: e.activation(out=st1[:, 1:2], in_=st1[:, 1:2],
                func=AF.Ln), reads=(t_st1,), writes=(t_st1,))
            act(lambda e: e.activation(out=st1[:, 1:2], in_=st1[:, 1:2],
                func=AF.Exp, scale=-0.5), reads=(t_st1,), writes=(t_st1,))
            dve(lambda e, xs=xsl: e.tensor_scalar(
                out=xnstg[:], in0=xs, scalar1=st1[:, 0:1], scalar2=st1[:, 1:2],
                op0=ALU.subtract, op1=ALU.mult),
                reads=(t_xt, t_st1), writes=(t_xns,))
            for c in range(8):
                tb = TBk[:, 128 * (c % 8):128 * (c % 8 + 1)]
                tr(tb, xnstg[:, 128 * c:128 * (c + 1)], self.identb[:],
                   reads=(t_xns, self.t_identb), writes=(tTB,))
                dve(lambda e, tb=tb, c=c, tt=tt: e.tensor_copy(
                    xnT[:, 2052 * c + 1 + 128 * tt:2052 * c + 1 + 128 * (tt + 1)],
                    tb), reads=(tTB,), writes=(t_xnT,))

        dve(lambda e: e.memset(wld[:, 0:1024], 0.0), writes=(t_wld,))
        for wi, nm in enumerate(("L0_attn_w1_write", "L0_attn_w2_write",
                                 "L0_attn_w1_read", "L0_attn_w2_read")):
            dma("pool", wld[0:64, 0:1024], self.din[nm], writes=(t_wld,))
            for c in range(8):
                tb = TBk[:, 128 * (c % 8):128 * (c % 8 + 1)]
                tr(tb, wld[:, 128 * c:128 * (c + 1)], self.identb[:],
                   reads=(t_wld, self.t_identb), writes=(tTB,))
                dve(lambda e, tb=tb, c=c, wi=wi: e.tensor_copy(
                    w4T[:, 256 * c + 64 * wi:256 * c + 64 * (wi + 1)],
                    tb[:, 0:64]), reads=(tTB,), writes=(t_w4,))

        ev4 = extb[:].rearrange("p (t w h q) -> p t w h q", w=4, h=16, q=4)
        lwv = linw[:].rearrange("p (t h c) -> p t h c", h=16, c=6)
        lrv = linr[:].rearrange("p (t h c) -> p t h c", h=16, c=6)
        lsv = lst[:].rearrange("p (t h c) -> p t h c", h=16, c=6)
        tmv = tm1[:].rearrange("p (t h) -> p t h", h=16)
        for ch in range(2):
            for i8 in range(8):
                tt = 8 * ch + i8
                pb, pbank = tP[tt % 2], P[tt % 2]
                for c in range(8):
                    self.mm(pbank[:, 0:64],
                            xnT[:, 2052 * c + 128 * tt:2052 * c + 128 * (tt + 1)],
                            w4T[:, 256 * c:256 * c + 64],
                            c == 0, c == 7, reads=(t_xnT, t_w4), writes=(pb,))
                for c in range(8):
                    self.mm(pbank[:, 64:256],
                            xnT[:, 2052 * c + 1 + 128 * tt:
                                2052 * c + 1 + 128 * (tt + 1)],
                            w4T[:, 256 * c + 64:256 * c + 256],
                            c == 0, c == 7, reads=(t_xnT, t_w4), writes=(pb,))
                dve(lambda e, i8=i8, pk=pbank: e.tensor_copy(
                    extb[:, 256 * i8:256 * (i8 + 1)], pk[:, 0:256]),
                    reads=(pb,), writes=(t_ext,))
            for (lv, t_lo, b1, b2, fold) in ((lwv, t_lw, 0, 1, False),
                                             (lrv, t_lr, 2, 3, True)):
                for cpos in range(6):
                    jj = (5 - cpos) if fold else cpos
                    swap = fold and SGN[cpos] < 0
                    pi, pj = PI[jj], PJ[jj]
                    dve(lambda e, i0=ev4[:, :, b1, :, pi], i1=ev4[:, :, b2, :, pj]:
                        e.tensor_tensor(out=tmv, in0=i0, in1=i1, op=ALU.mult),
                        reads=(t_ext,), writes=(t_tm1,))
                    dve(lambda e, o=lsv[:, :, :, cpos], i0=ev4[:, :, b1, :, pj],
                        i1=ev4[:, :, b2, :, pi]:
                        e.tensor_tensor(out=o, in0=i0, in1=i1, op=ALU.mult),
                        reads=(t_ext,), writes=(t_lst,))
                    if swap:
                        dve(lambda e, o=lsv[:, :, :, cpos], i0=lsv[:, :, :, cpos]:
                            e.tensor_tensor(out=o, in0=i0, in1=tmv,
                                            op=ALU.subtract),
                            reads=(t_lst, t_tm1), writes=(t_lst,))
                    else:
                        dve(lambda e, o=lsv[:, :, :, cpos], i1=lsv[:, :, :, cpos]:
                            e.tensor_tensor(out=o, in0=tmv, in1=i1,
                                            op=ALU.subtract),
                            reads=(t_lst, t_tm1), writes=(t_lst,))
                dve(lambda e: e.tensor_tensor(out=l2s[:], in0=lst[:], in1=lst[:],
                    op=ALU.mult), reads=(t_lst,), writes=(t_l2s,))
                dve(lambda e: e.tensor_reduce(
                    out=nst[:].rearrange("p (t h o) -> p t h o", h=16, o=1),
                    in_=l2s[:].rearrange("p (t h c) -> p t h c", h=16, c=6),
                    axis=AX.X, op=ALU.add), reads=(t_l2s,), writes=(t_nst,))
                dve(lambda e: e.tensor_scalar(out=nst[:], in0=nst[:],
                    scalar1=1e-24, scalar2=None, op0=ALU.max),
                    reads=(t_nst,), writes=(t_nst,))
                act(lambda e: e.activation(out=nst[:], in_=nst[:], func=AF.Ln),
                    reads=(t_nst,), writes=(t_nst,))
                act(lambda e: e.activation(out=nst[:], in_=nst[:], func=AF.Exp,
                    scale=-0.5), reads=(t_nst,), writes=(t_nst,))
                dve(lambda e, o=lv[:, 8 * ch:8 * ch + 8, :, :]:
                    e.tensor_tensor(
                        out=o, in0=lsv[:, :, :, :],
                        in1=nst[:].rearrange("p (t h o) -> p t h o", h=16, o=1
                                             ).broadcast_to([128, 8, 16, 6]),
                        op=ALU.mult), reads=(t_lst, t_nst), writes=(t_lo,))

        self.dbg_dump("xnT", b, xnT[:], t_xnT)
        self.dbg_dump("linw", b, linw[:], t_lw)
        self.dbg_dump("linr", b, linr[:], t_lr)
        if self.stage == "lines":
            return

        ctx = dict(b=b, xnT=xnT, t_xnT=t_xnT, wqT=wqT, t_wq=t_wq,
                   wld=wld, t_wld=t_wld, t_wldS=t_wldS,
                   linw=linw, t_lw=t_lw, linr=linr, t_lr=t_lr,
                   qt_=qt_, t_qt=t_qt, kt_=kt_, t_kt=t_kt, vp_=vp_, t_vp=t_vp,
                   rtg=rtg, t_rtg=t_rtg, jwb=jwb, t_jwb=t_jwb,
                   rpp=rpp, t_rpp=t_rpp, jpp=jpp, t_jpp=t_jpp,
                   ebuf=ebuf, t_eb=t_eb, wdg=wdg, t_wdg=t_wdg,
                   trs=trs, t_trs=t_trs, negc=negc, t_negc=t_negc,
                   les=les, t_les=t_les, nc4=nc4, t_nc4=t_nc4,
                   ptb=ptb, t_ptb=t_ptb, t_ptbS=[self.trk() for _ in range(3)],
                   t_rppB=[self.trk() for _ in range(4)],
                   dum=dum, t_dum=t_dum,
                   tdg=tdg, t_tdg=t_tdg, avu=avu, t_avu=t_avu,
                   lin=lin, t_lin=t_lin, ldg=ldg, t_ldg=t_ldg,
                   bsg=bsg, t_bsg=t_bsg, lqb=lqb, t_lqb=t_lqb,
                   st1=st1, t_st1=t_st1, ldbc=ldbc, t_ldbc=t_ldbc)
        for g in range(4):
            self.group(g, ctx)
        self.dbg_dump("avt", b, self.avt[:], self.t_avt)
        self._wbufs = ((qt_, t_qt), (kt_, t_kt), (rtg, t_rtg), (jwb, t_jwb),
                       (wld, t_wld), (xt_, t_xt))

    def group(self, g, c):
        S, nc = self.S, self.nc
        pe, dve, act, dma, mm, tr = (self.pe, self.dve, self.act, self.dma,
                                     self.mm, self.tr)
        P, tP, TBk, tTB = self.P, self.tP, self.TBk, self.tTB
        b = c["b"]

        dve(lambda e: e.memset(c["ldg"][:], 0.0), writes=(c["t_ldg"],))
        dve(lambda e: e.memset(c["bsg"][:], 1.0), writes=(c["t_bsg"],))
        for h2 in range(4):
            dma("sync", c["ldg"][32 * h2:32 * h2 + 6, 0:1],
                self.ldrow[0:1, 4 * g + h2:4 * g + h2 + 1].broadcast_to([1, 6]),
                reads=(self.t_ldrow,), writes=(c["t_ldg"],))
            dma("sync", c["bsg"][32 * h2:32 * h2 + 6, 0:1],
                self.bsrow[0:1, 4 * g + h2:4 * g + h2 + 1].broadcast_to([1, 6]),
                reads=(self.t_bsrow,), writes=(c["t_bsg"],))

        for (src, t_src, dstt, t_dst) in (
                (c["linw"], c["t_lw"], c["jwb"], c["t_jwb"]),
                (c["linr"], c["t_lr"], c["rtg"], c["t_rtg"])):
            dve(lambda e: e.memset(c["trs"][0:32, :], 1.0),
                writes=(c["t_trs"],))
            for tt in range(NT):
                tb24 = TBk[0:24, 128 * (tt % 8):128 * (tt % 8 + 1)]
                tr(tb24, src[:, 96 * tt + 24 * g:96 * tt + 24 * (g + 1)],
                   self.identb[:], reads=(t_src, self.t_identb),
                   writes=(tTB,))
                dve(lambda e, tb24=tb24, tt=tt: e.tensor_copy(
                    c["trs"][0:24, 128 * tt:128 * (tt + 1)],
                    tb24), reads=(tTB,), writes=(c["t_trs"],))
            for h2 in range(4):
                dma("sync", dstt[32 * h2:32 * h2 + 6, :],
                    c["trs"][6 * h2:6 * h2 + 6, :],
                    reads=(c["t_trs"],), writes=(t_dst,))
            dma("sync", dstt[6:7, 0:2048] if False else dstt[0:0, 0:0],
                c["trs"][0:0, :]) if False else None
            for h2 in range(4):
                dma("sync", dstt[32 * h2 + 6:32 * h2 + 7, :],
                    c["trs"][24:25, :],
                    reads=(c["t_trs"],), writes=(t_dst,))
        dve(lambda e: e.tensor_scalar(out=c["jwb"][:], in0=c["jwb"][:],
            scalar1=c["bsg"][:, 0:1], scalar2=None, op0=ALU.mult),
            reads=(c["t_jwb"], c["t_bsg"]), writes=(c["t_jwb"],))
        act(lambda e: e.activation(out=c["ebuf"][:, 0:1536], in_=self.eqpat[:, 0:1536],
            func=AF.Exp, scale=c["ldg"][:, 0:1], bias=0.0),
            reads=(self.t_eqpat, c["t_ldg"]), writes=(c["t_eb"],))
        dve(lambda e: e.tensor_tensor(out=c["rpp"][:, 0:1536], in0=c["rtg"][:, 0:1536],
            in1=c["ebuf"][:, 0:1536], op=ALU.mult),
            reads=(c["t_rtg"], c["t_eb"]),
            writes=(c["t_rppB"][0], c["t_rppB"][1], c["t_rppB"][2]))
        act(lambda e: e.activation(out=c["ebuf"][:, 0:512],
            in_=self.eqpat[:, 1536:2048],
            func=AF.Exp, scale=c["ldg"][:, 0:1], bias=0.0),
            reads=(self.t_eqpat, c["t_ldg"]), writes=(c["t_eb"],))
        dve(lambda e: e.tensor_tensor(out=c["rpp"][:, 1536:2048],
            in0=c["rtg"][:, 1536:2048], in1=c["ebuf"][:, 0:512], op=ALU.mult),
            reads=(c["t_rtg"], c["t_eb"]), writes=(c["t_rppB"][3],))
        for qb in (1, 2, 3):
            n = 512 * qb
            dve(lambda e, qb=qb: e.tensor_scalar(out=c["lqb"][:], in0=c["ldg"][:],
                scalar1=512.0 * qb, scalar2=None, op0=ALU.mult),
                reads=(c["t_ldg"],), writes=(c["t_lqb"],))
            act(lambda e, n=n: e.activation(out=c["ebuf"][:, 0:n],
                in_=self.kneg[:, 0:n], func=AF.Exp,
                scale=c["ldg"][:, 0:1], bias=c["lqb"][:, 0:1]),
                reads=(self.t_kneg, c["t_ldg"], c["t_lqb"]), writes=(c["t_eb"],))
            dve(lambda e, qb=qb, n=n: e.tensor_tensor(
                out=c["jpp"][:, 1536 * (qb - 1):1536 * (qb - 1) + n],
                in0=c["jwb"][:, 0:n], in1=c["ebuf"][:, 0:n], op=ALU.mult),
                reads=(c["t_jwb"], c["t_eb"]), writes=(c["t_jpp"],))

        self.dbg_dump(f"rtg{g}", b, c["rtg"][:], c["t_rtg"])
        self.dbg_dump(f"jwb{g}", b, c["jwb"][:], c["t_jwb"])
        self.dbg_dump(f"rpp{g}", b, c["rpp"][:], c["t_rppB"][3])
        self.dbg_dump(f"jpp{g}", b, c["jpp"][:], c["t_jpp"])

        for hp in range(2):
            self.headpair(g, hp, c)

    def headpair(self, g, hp, c):
        pe, dve, act, dma, mm, tr = (self.pe, self.dve, self.act, self.dma,
                                     self.mm, self.tr)
        P, tP, TBk, tTB = self.P, self.tP, self.TBk, self.tTB
        b = c["b"]
        hpg = 2 * g + hp
        xnT, t_xnT = c["xnT"], c["t_xnT"]
        wqT, t_wq = c["wqT"], c["t_wq"]
        qt_, kt_, vp_ = c["qt_"], c["kt_"], c["vp_"]
        t_qt, t_kt, t_vp = c["t_qt"], c["t_kt"], c["t_vp"]

        wld, t_wld = c["wld"], c["t_wld"]
        t_wldS = c["t_wldS"]
        slot = 3072 * (hpg % 2)
        wsl = lambda k, wi: (3 * k + wi) % 4
        if not self._wld_pref:
            for wi, rbase in enumerate((128 * hpg, D + 128 * hpg,
                                        2 * D + 128 * hpg)):
                s4 = wsl(hpg, wi)
                dma("pool", wld[:, 1024 * s4:1024 * (s4 + 1)],
                    self.din["L0_attn_w_qkv"][rbase:rbase + 128, :],
                    writes=(t_wldS[s4], t_wld))
            self._wld_pref = True
        for wi in range(3):
            s4 = wsl(hpg, wi)
            dst = wld[:, 1024 * s4:1024 * (s4 + 1)]
            for ci in range(8):
                tb = TBk[:, 128 * (ci % 8):128 * (ci % 8 + 1)]
                tr(tb, dst[:, 128 * ci:128 * (ci + 1)], self.identb[:],
                   reads=(t_wldS[s4], t_wld, self.t_identb), writes=(tTB,))
                dve(lambda e, tb=tb, ci=ci, wi=wi: e.tensor_copy(
                    wqT[:, slot + 384 * ci + 128 * wi:slot + 384 * ci + 128 * (wi + 1)],
                    tb), reads=(tTB,), writes=(t_wq[(hpg % 2) * 8 + ci],))
        if hpg < 7:
            for wi, rbase in enumerate((128 * (hpg + 1), D + 128 * (hpg + 1),
                                        2 * D + 128 * (hpg + 1))):
                s4 = wsl(hpg + 1, wi)
                dma("pool", wld[:, 1024 * s4:1024 * (s4 + 1)],
                    self.din["L0_attn_w_qkv"][rbase:rbase + 128, :],
                    writes=(t_wldS[s4],))

        for (wi, base_row, dst, t_dst, scale) in (
                (0, 128 * hpg, qt_, t_qt, 0.125),
                (1, D + 128 * hpg, kt_, t_kt, 1.0)):
            for seg in range(4):
                pb, pbank = tP[seg % 2], P[seg % 2]
                self.mm(pbank[:, 0:512],
                        self.brow[0:1, base_row:base_row + 128],
                        self.onesb[0:1, 0:512], True, False,
                        reads=(self.t_brow, self.t_onesb), writes=(pb,))
                for ci in range(8):
                    self.mm(pbank[:, 0:512],
                            wqT[:, slot + 384 * ci + 128 * wi:
                                slot + 384 * ci + 128 * (wi + 1)],
                            xnT[:, 2052 * ci + 1 + 512 * seg:
                                2052 * ci + 1 + 512 * (seg + 1)],
                            False, ci == 7,
                            reads=(t_wq[(hpg % 2) * 8 + ci], t_xnT),
                            writes=(pb,))
                act(lambda e, pk=pbank, dst=dst, seg=seg, sc=scale: e.activation(
                    out=dst[:, 512 * seg:512 * (seg + 1)], in_=pk[:, 0:512],
                    func=AF.Identity, scale=sc),
                    reads=(pb,), writes=(t_dst,))
        vstg = c["ptb"]
        t_vstg = tuple(c["t_ptbS"]) + (c["t_ptb"],)
        base_row = 2 * D + 128 * hpg
        for seg in range(4):
            pb, pbank = tP[seg % 2], P[seg % 2]
            self.mm(pbank[:, 0:512],
                    self.brow[0:1, base_row:base_row + 128],
                    self.onesb[0:1, 0:512], True, False,
                    reads=(self.t_brow, self.t_onesb), writes=(pb,))
            for ci in range(8):
                self.mm(pbank[:, 0:512],
                        wqT[:, slot + 384 * ci + 256:slot + 384 * ci + 384],
                        xnT[:, 2052 * ci + 1 + 512 * seg:
                            2052 * ci + 1 + 512 * (seg + 1)],
                        False, ci == 7,
                        reads=(t_wq[(hpg % 2) * 8 + ci], t_xnT),
                        writes=(pb,))
            act(lambda e, pk=pbank, seg=seg: e.activation(
                out=vstg[:, 512 * seg:512 * (seg + 1)], in_=pk[:, 0:512],
                func=AF.Identity), reads=(pb,), writes=t_vstg)
        for tt in range(NT):
            dve(lambda e, tt=tt: e.memset(
                vp_[:, 132 * tt + 64:132 * tt + 65], 1.0), writes=(t_vp,))
            dve(lambda e, tt=tt: e.memset(
                vp_[:, 132 * tt + 130:132 * tt + 131], 1.0), writes=(t_vp,))
            tb = TBk[:, 128 * (tt % 8):128 * (tt % 8 + 1)]
            tr(tb, vstg[:, 128 * tt:128 * (tt + 1)], self.identb[:],
               reads=t_vstg + (self.t_identb,), writes=(tTB,))
            dve(lambda e, tb=tb, tt=tt: e.tensor_copy(
                vp_[:, 132 * tt + 0:132 * tt + 64], tb[:, 0:64]),
                reads=(tTB,), writes=(t_vp,))
            dve(lambda e, tb=tb, tt=tt: e.tensor_copy(
                vp_[:, 132 * tt + 66:132 * tt + 130], tb[:, 64:128]),
                reads=(tTB,), writes=(t_vp,))

        if self.stage == "qkv" and hpg == 0:
            self.dbg_dump("qt", b, qt_[:], t_qt)
            self.dbg_dump("kt", b, kt_[:], t_kt)
            self.dbg_dump("vp", b, vp_[:], t_vp)
        if self.stage == "qkv":
            return

        for h2 in range(2):
            h = 4 * g + 2 * hp + h2
            for cc in range(4):
                sl = c["wdg"][:, 1024 * cc + 512 * h2:1024 * cc + 512 * (h2 + 1)]
                act(lambda e, sl=sl, cc=cc, h=h: e.activation(
                    out=sl, in_=self.diffc[:, 512 * cc:512 * (cc + 1)],
                    func=AF.Exp, scale=c["ldbc"][:, h:h + 1]),
                    reads=(self.t_diffc, c["t_ldbc"]), writes=(c["t_wdg"],))
                dve(lambda e, sl=sl, cc=cc: e.tensor_tensor(
                    out=sl, in0=sl, in1=self.mc[:, 512 * cc:512 * (cc + 1)],
                    op=ALU.mult),
                    reads=(c["t_wdg"], self.t_mc), writes=(c["t_wdg"],))

        c["rot"] = 0
        st0 = self.stats_steps(g, hp, 0, c)
        for s in st0:
            s()
        for qb in range(4):
            extra = self.stats_steps(g, hp, qb + 1, c) if qb < 3 else []
            self.kloop(g, hp, qb, c, extra)

    def stats_steps(self, g, hp, qb, c):
        dve, act, dma = self.dve, self.act, self.dma
        P, P2, tP = self.P, self.P2, self.tP
        t_qt, t_kt = c["t_qt"], c["t_kt"]
        SL = (0, 1, 3)
        steps = []
        for qt in range(4 * qb, 4 * qb + 4):
            for kb in range(qb + 1):
                def s_emit(qt=qt, kb=kb):
                    sl = SL[c["rot"] % 3]
                    c["rot"] += 1
                    trks = (tP[2 * sl], tP[2 * sl + 1])
                    for h2 in range(2):
                        hrow = 64 * h2
                        self.mm(P[2 * sl + h2][:, 0:512],
                                c["qt_"][hrow:hrow + 64, 128 * qt:128 * (qt + 1)],
                                c["kt_"][hrow:hrow + 64, 512 * kb:512 * (kb + 1)],
                                True, True, reads=(t_qt, t_kt),
                                writes=(trks[h2],))
                    for h2 in range(2):
                        act(lambda e, pk=P[2 * sl + h2], h2=h2, kb=kb:
                            e.activation(
                                out=c["dum"][:], in_=pk[:, 0:512], func=AF.Exp,
                                scale=0.125,
                                accum_out=c["les"][:, 8 * h2 + kb:
                                                   8 * h2 + kb + 1]),
                            reads=(trks[h2],), writes=(c["t_dum"], c["t_les"]))
                    self.fill(reads=(c["t_les"],),
                              bank=2 * SL[c["rot"] % 3])
                steps.append(s_emit)
            def c_emit(qt=qt):
                for h2 in range(2):
                    colc = 4 * h2 + (qt % 4)
                    dve(lambda e, h2=h2: e.tensor_reduce(
                        out=c["st1"][:, 8:9],
                        in_=c["les"][:, 8 * h2:8 * h2 + qb + 1],
                        axis=AX.X, op=ALU.add),
                        reads=(c["t_les"],), writes=(c["t_st1"],))
                    act(lambda e: e.activation(out=c["st1"][:, 8:9],
                        in_=c["st1"][:, 8:9], func=AF.Ln),
                        reads=(c["t_st1"],), writes=(c["t_st1"],))
                    dve(lambda e, colc=colc: e.tensor_scalar(
                        out=c["nc4"][:, colc:colc + 1], in0=c["st1"][:, 8:9],
                        scalar1=-8.0, scalar2=CSHIFT, op0=ALU.mult,
                        op1=ALU.add),
                        reads=(c["t_st1"],), writes=(c["t_nc4"],))
            steps.append(c_emit)
        for h2 in range(2):
            def f_emit(h2=h2):
                sl = SL[c["rot"] % 3]
                c["rot"] += 1
                pb, pbank = tP[2 * sl + h2], P[2 * sl + h2]
                nsl = 512 * (qb % 2)
                for qi in range(4):
                    self.mm(pbank[0:1, 128 * qi:128 * (qi + 1)],
                            c["nc4"][:, 4 * h2 + qi:4 * h2 + qi + 1],
                            self.identb[:], True, True,
                            reads=(c["t_nc4"], self.t_identb), writes=(pb,))
                dve(lambda e, pk=pbank, h2=h2, nsl=nsl: e.tensor_copy(
                    c["negc"][64 * h2:64 * h2 + 1, nsl:nsl + 512],
                    pk[0:1, 0:512]), reads=(pb,), writes=(c["t_negc"],))
                h2g = 2 * hp + h2
                dma("sync",
                    c["rpp"][32 * h2g + 6:32 * h2g + 7, 512 * qb:512 * (qb + 1)],
                    c["negc"][64 * h2:64 * h2 + 1, nsl:nsl + 512],
                    reads=(c["t_negc"],), writes=(c["t_rppB"][qb],))
            steps.append(f_emit)
        return steps

    def kloop(self, g, hp, qb, c, extra=()):
        pe, dve, act, dma, mm, tr = (self.pe, self.dve, self.act, self.dma,
                                     self.mm, self.tr)
        P, P2, tP = self.P, self.P2, self.tP
        qt_, kt_, vp_ = c["qt_"], c["kt_"], c["vp_"]
        t_qt, t_kt, t_vp = c["t_qt"], c["t_kt"], c["t_vp"]
        nkc = 4 * qb + 4
        extra = list(extra)
        nsl_rd = 512 * (qb % 2)
        ei = 0
        SL = (0, 1, 3)
        kc_slot = {}

        def logits_phase(kc):
            diag = kc >= 4 * qb
            sl = SL[c["rot"] % 3]
            c["rot"] += 1
            kc_slot[kc] = sl
            pl2 = P2[sl][:]
            trks = (tP[2 * sl], tP[2 * sl + 1])
            ptsl = c["ptb"][:, 1024 * (kc % 2):1024 * (kc % 2 + 1)]
            tpt = c["t_ptbS"][kc % 2]
            if not diag:
                for h2 in range(2):
                    h2g = 2 * hp + h2
                    self.mm(P[2 * sl + h2][:, 0:512],
                            c["jpp"][32 * h2g:32 * h2g + 7,
                                     1536 * (qb - 1) + 128 * kc:
                                     1536 * (qb - 1) + 128 * (kc + 1)],
                            c["rpp"][32 * h2g:32 * h2g + 7,
                                     512 * qb:512 * (qb + 1)],
                            True, False, reads=(c["t_jpp"], c["t_rppB"][qb]),
                            writes=(trks[h2],),
                            tp=(96, 0) if h2g == 3 else None)
                for h2 in range(2):
                    hrow = 64 * h2
                    self.mm(P[2 * sl + h2][:, 0:512],
                            kt_[hrow:hrow + 64, 128 * kc:128 * (kc + 1)],
                            qt_[hrow:hrow + 64, 512 * qb:512 * (qb + 1)],
                            False, True, reads=(t_kt, t_qt),
                            writes=(trks[h2],))
                act(lambda e, o=ptsl, pk=pl2: e.activation(
                    out=o, in_=pk, func=AF.Exp),
                    reads=trks, writes=(tpt,))
            else:
                cc = kc - 4 * qb
                sx = SL[c["rot"] % 3]
                c["rot"] += 1
                px2 = P2[sx][:]
                xtrks = (tP[2 * sx], tP[2 * sx + 1])
                for h2 in range(2):
                    h2g = 2 * hp + h2
                    self.mm(P[2 * sx + h2][:, 0:512],
                            c["jwb"][32 * h2g:32 * h2g + 6,
                                     128 * kc:128 * (kc + 1)],
                            c["rtg"][32 * h2g:32 * h2g + 6,
                                     512 * qb:512 * (qb + 1)],
                            True, True, reads=(c["t_jwb"], c["t_rtg"]),
                            writes=(xtrks[h2],),
                            tp=(96, 0) if h2g == 3 else None)
                dve(lambda e, pk=px2, cc=cc: e.tensor_tensor(
                    out=c["tdg"][:], in0=pk,
                    in1=c["wdg"][:, 1024 * cc:1024 * (cc + 1)],
                    op=ALU.mult),
                    reads=xtrks + (c["t_wdg"],), writes=(c["t_tdg"],))
                for h2 in range(2):
                    hrow = 64 * h2
                    self.mm(P[2 * sl + h2][:, 0:512],
                            kt_[hrow:hrow + 64, 128 * kc:128 * (kc + 1)],
                            qt_[hrow:hrow + 64, 512 * qb:512 * (qb + 1)],
                            True, False, reads=(t_kt, t_qt),
                            writes=(trks[h2],))
                for h2 in range(2):
                    self.mm(P[2 * sl + h2][:, 0:512],
                            self.onesb[64 * h2:64 * h2 + 1, 0:128],
                            c["negc"][64 * h2:64 * h2 + 1, nsl_rd:nsl_rd + 512],
                            False, True, reads=(self.t_onesb, c["t_negc"]),
                            writes=(trks[h2],))
                dve(lambda e, pk=pl2: e.tensor_tensor(
                    out=pk, in0=pk, in1=c["tdg"][:], op=ALU.add),
                    reads=trks + (c["t_tdg"],), writes=trks)
                act(lambda e, o=ptsl, pk=pl2: e.activation(
                    out=o, in_=pk, func=AF.Exp),
                    reads=trks, writes=(tpt,))
                dve(lambda e, o=ptsl, cc=cc: e.tensor_tensor(
                    out=o.rearrange("p (h f) -> p h f", h=2),
                    in0=o.rearrange("p (h f) -> p h f", h=2),
                    in1=self.m2c[:, 512 * cc:512 * (cc + 1)].rearrange(
                        "p (o f) -> p o f", o=1).broadcast_to([128, 2, 512]),
                    op=ALU.mult),
                    reads=(tpt, self.t_m2c), writes=(tpt,))
            fsl = SL[c["rot"] % 3]
            self.mm(P[2 * fsl][0:1, 0:512], self.onesb[0:1, 0:1],
                    self.onesb[0:1, 0:512], True, True,
                    reads=(self.t_onesb,), writes=(tP[2 * fsl],))

        def pv_phase(kc):
            for h2 in range(2):
                self.mm(P[4 + h2][0:65, 0:512],
                        vp_[:, 132 * kc:132 * kc + 65] if h2 == 0 else
                        vp_[:, 132 * kc + 66:132 * kc + 131],
                        c["ptb"][:, 1024 * (kc % 2) + 512 * h2:
                                 1024 * (kc % 2) + 512 * (h2 + 1)],
                        kc == 0, kc == nkc - 1,
                        reads=(t_vp, c["t_ptbS"][kc % 2]), writes=(tP[4 + h2],))

        for step in range(nkc + 1):
            if step < nkc:
                logits_phase(step)
            want = min(len(extra), ((step + 1) * len(extra)) // max(nkc - 1, 1))
            while ei < want:
                extra[ei]()
                ei += 1
            if step >= 1:
                pv_phase(step - 1)
        while ei < len(extra):
            extra[ei]()
            ei += 1

        dve(lambda e: e.tensor_copy(c["avu"][0:64, :], P2[2][:][0:64, :]),
            reads=(tP[4], tP[5]), writes=(c["t_avu"],))
        act(lambda e: e.activation(out=c["lin"][64:65, :], in_=P2[2][:][64:65, :],
            func=AF.Ln), reads=(tP[4], tP[5]), writes=(c["t_lin"],))
        act(lambda e: e.activation(out=c["lin"][64:65, :],
            in_=c["lin"][64:65, :], func=AF.Exp, scale=-1.0),
            reads=(c["t_lin"],), writes=(c["t_lin"],))
        for h2 in range(2):
            self.mm(P[4 + h2][0:64, 0:512], self.onesb[64:65, 0:64],
                    c["lin"][64:65, 512 * h2:512 * (h2 + 1)], True, True,
                    reads=(self.t_onesb, c["t_lin"]), writes=(tP[4 + h2],))
        for h2 in range(2):
            h = 4 * g + 2 * hp + h2
            ch = h // 2
            rr = 64 * (h % 2)
            dve(lambda e, h2=h2, ch=ch, rr=rr, qb=qb: e.tensor_tensor(
                out=self.avt[rr:rr + 64,
                             2048 * ch + 512 * qb:2048 * ch + 512 * (qb + 1)],
                in0=c["avu"][0:64, 512 * h2:512 * (h2 + 1)],
                in1=P[4 + h2][0:64, 0:512],
                op=ALU.mult),
                reads=(c["t_avu"], tP[4 + h2]), writes=(self.t_avt,))

    def wout(self, b):
        pe, dve, act, dma, mm, tr = (self.pe, self.dve, self.act, self.dma,
                                     self.mm, self.tr)
        P, tP, TBk, tTB = self.P, self.tP, self.TBk, self.tTB
        nc = self.nc
        dx = self.din["L0_attn_x"]

        (wsl0, wsl1, wsl2, wsl3, wldp, xtp) = self._wbufs
        wslabs = [wsl0, wsl1, wsl2, wsl3]
        wld2, t_wl2 = wldp
        xt2, t_xt2 = xtp
        for rc in range(8):
            dst = wld2[:, 1024 * (rc % 2):1024 * (rc % 2 + 1)]
            dma("pool", dst, self.din["L0_attn_w_out"][128 * rc:128 * (rc + 1), :],
                writes=(t_wl2,))
            for ci in range(8):
                tb = TBk[:, 128 * (ci % 8):128 * (ci % 8 + 1)]
                tr(tb, dst[:, 128 * ci:128 * (ci + 1)], self.identb[:],
                   reads=(t_wl2, self.t_identb), writes=(tTB,))
                wt, t_wt = wslabs[ci // 2]
                dve(lambda e, tb=tb, ci=ci, rc=rc, wt=wt: e.tensor_copy(
                    wt[:, 1024 * (ci % 2) + 128 * rc:
                       1024 * (ci % 2) + 128 * (rc + 1)],
                    tb), reads=(tTB,), writes=(t_wt,))
        for tt in range(NT):
            xsl = xt2[:, 1024 * (tt % 2):1024 * (tt % 2 + 1)]
            dma("sync", xsl, dx[b, 128 * tt:128 * (tt + 1), :], writes=(t_xt2,))
            for dh in range(2):
                pb, pbank = tP[(2 * tt + dh) % 2], P[(2 * tt + dh) % 2]
                self.mm(pbank[:, 0:512], self.onesb[0:1, 0:128],
                        self.borow[0:1, 512 * dh:512 * (dh + 1)], True, False,
                        reads=(self.t_onesb, self.t_borow), writes=(pb,))
                for ci in range(8):
                    wt, t_wt = wslabs[ci // 2]
                    self.mm(pbank[:, 0:512],
                            self.avt[:, 2048 * ci + 128 * tt:2048 * ci + 128 * (tt + 1)],
                            wt[:, 1024 * (ci % 2) + 512 * dh:
                               1024 * (ci % 2) + 512 * (dh + 1)],
                            False, ci == 7,
                            reads=(self.t_avt, t_wt), writes=(pb,))
                dve(lambda e, pk=pbank, xs=xsl, dh=dh: e.tensor_tensor(
                    out=xs[:, 512 * dh:512 * (dh + 1)],
                    in0=xs[:, 512 * dh:512 * (dh + 1)], in1=pk[:, 0:512],
                    op=ALU.add), reads=(pb, t_xt2), writes=(t_xt2,))
            dma("sync", self.dscr[b, 128 * tt:128 * (tt + 1), :], xsl,
                reads=(t_xt2,), writes=(self.t_dscr,))

    def ffn(self, b):
        pe, dve, act, dma, mm, tr = (self.pe, self.dve, self.act, self.dma,
                                     self.mm, self.tr)
        P, tP, TBk, tTB = self.P, self.tP, self.TBk, self.tTB
        nc = self.nc

        fx = contextlib.ExitStack()
        sbf = lambda n, s, d: self.sb(fx, f"{n}{b}", s, d)
        oac, t_oac = sbf("oac", [128, 16 * 1024], F32)
        xn2T, t_x2 = sbf("xn2T", [128, 8 * 2048], BF16)
        sel32, t_sel32 = sbf("sel32", [32, 32 * 128], BF16)
        selstg, t_sst = sbf("selstg", [32, 1024], BF16)
        selT, t_selT = sbf("selT", [128, 8 * 32], BF16)
        gates, t_g = sbf("gates", [128, 16 * 32], F32)
        gwork, t_gw = sbf("gwork", [128, 32], F32)
        gind, t_gi = sbf("gind", [128, 32], F32)
        gwb, t_gwb = sbf("gwb", [128, 32], BF16)
        gT, t_gT = sbf("gT", [32, 2048], BF16)
        gb, t_gb = sbf("gb", [128, 2 * 2048], BF16)
        spr, t_spr0 = sbf("spr", [128, 2 * 8192], BF16)
        t_spr = [t_spr0, Trk(self.bar)]
        kbf, t_kbf0 = sbf("kbf", [128, 2 * 4096], BF16)
        t_kbf = [t_kbf0, Trk(self.bar)]
        vbf, t_vbf0 = sbf("vbf", [128, 2 * 4096], BF16)
        t_vbf = [t_vbf0, Trk(self.bar)]
        xn2s, t_x2s = sbf("xn2s", [128, 1024], BF16)
        st2, t_st2 = sbf("st2", [128, 16], F32)

        dma("sync", sel32[:], self.din["c_sel32"], writes=(t_sel32,))
        dma("pool", selstg[:], self.din["L0_ffn_expert_sel"], writes=(t_sst,))
        for ci in range(8):
            tb = TBk[:, 128 * (ci % 8):128 * (ci % 8) + 32]
            tr(tb, selstg[:, 128 * ci:128 * (ci + 1)], self.identb[0:32, 0:32],
               reads=(t_sst, self.t_identb), writes=(tTB,))
            dve(lambda e, tb=tb, ci=ci: e.tensor_copy(
                selT[:, 32 * ci:32 * (ci + 1)], tb),
                reads=(tTB,), writes=(t_selT,))

        for tt in range(NT):
            osl = oac[:, 1024 * tt:1024 * (tt + 1)]
            dma("sync", osl, self.dscr[b, 128 * tt:128 * (tt + 1), :],
                reads=(self.t_dscr,), writes=(t_oac,))
            dve(lambda e, xs=osl: e.tensor_reduce(
                out=st2[:, 0:1], in_=xs, axis=AX.X, op=ALU.add),
                reads=(t_oac,), writes=(t_st2,))
            dve(lambda e: e.tensor_scalar(out=st2[:, 0:1], in0=st2[:, 0:1],
                scalar1=1.0 / D, scalar2=None, op0=ALU.mult),
                reads=(t_st2,), writes=(t_st2,))
            act(lambda e, xs=osl: e.activation(
                out=xn2s[:], in_=xs, func=AF.Square, accum_out=st2[:, 1:2]),
                reads=(t_oac,), writes=(t_x2s, t_st2))
            dve(lambda e: e.tensor_scalar(out=st2[:, 1:2], in0=st2[:, 1:2],
                scalar1=1.0 / D, scalar2=None, op0=ALU.mult),
                reads=(t_st2,), writes=(t_st2,))
            dve(lambda e: e.tensor_tensor(out=st2[:, 2:3], in0=st2[:, 0:1],
                in1=st2[:, 0:1], op=ALU.mult), reads=(t_st2,), writes=(t_st2,))
            dve(lambda e: e.tensor_tensor(out=st2[:, 1:2], in0=st2[:, 1:2],
                in1=st2[:, 2:3], op=ALU.subtract),
                reads=(t_st2,), writes=(t_st2,))
            dve(lambda e: e.tensor_scalar(out=st2[:, 1:2], in0=st2[:, 1:2],
                scalar1=1e-5, scalar2=None, op0=ALU.add),
                reads=(t_st2,), writes=(t_st2,))
            act(lambda e: e.activation(out=st2[:, 1:2], in_=st2[:, 1:2],
                func=AF.Ln), reads=(t_st2,), writes=(t_st2,))
            act(lambda e: e.activation(out=st2[:, 1:2], in_=st2[:, 1:2],
                func=AF.Exp, scale=-0.5), reads=(t_st2,), writes=(t_st2,))
            dve(lambda e, xs=osl: e.tensor_scalar(
                out=xn2s[:], in0=xs, scalar1=st2[:, 0:1], scalar2=st2[:, 1:2],
                op0=ALU.subtract, op1=ALU.mult),
                reads=(t_oac, t_st2), writes=(t_x2s,))
            for ci in range(8):
                tb = TBk[:, 128 * (ci % 8):128 * (ci % 8 + 1)]
                tr(tb, xn2s[:, 128 * ci:128 * (ci + 1)], self.identb[:],
                   reads=(t_x2s, self.t_identb), writes=(tTB,))
                dve(lambda e, tb=tb, ci=ci, tt=tt: e.tensor_copy(
                    xn2T[:, 2048 * ci + 128 * tt:2048 * ci + 128 * (tt + 1)],
                    tb), reads=(tTB,), writes=(t_x2,))
            pb, pbank = tP[4], P[4]
            for ci in range(8):
                self.mm(pbank[:, 0:32],
                        xn2T[:, 2048 * ci + 128 * tt:2048 * ci + 128 * (tt + 1)],
                        selT[:, 32 * ci:32 * (ci + 1)],
                        ci == 0, ci == 7, reads=(t_x2, t_selT), writes=(pb,))
            gsl = gates[:, 32 * tt:32 * (tt + 1)]
            act(lambda e, g_=gsl, pk=pbank: e.activation(
                out=g_, in_=pk[:, 0:32], func=AF.Sigmoid),
                reads=(pb,), writes=(t_g,))
            dve(lambda e, g_=gsl: e.tensor_copy(gwork[:], g_),
                reads=(t_g,), writes=(t_gw,))
            for rnd in range(4):
                dve(lambda e: e.tensor_reduce(out=st2[:, 3:4], in_=gwork[:],
                    axis=AX.X, op=ALU.max), reads=(t_gw,), writes=(t_st2,))
                if rnd < 3:
                    dve(lambda e: e.tensor_scalar(
                        out=gind[:], in0=gwork[:], scalar1=st2[:, 3:4],
                        scalar2=None, op0=ALU.is_ge),
                        reads=(t_gw, t_st2), writes=(t_gi,))
                    dve(lambda e: e.scalar_tensor_tensor(
                        out=gwork[:], in0=gind[:], scalar=-1e30, in1=gwork[:],
                        op0=ALU.mult, op1=ALU.add),
                        reads=(t_gi, t_gw), writes=(t_gw,))
            dve(lambda e, g_=gsl: e.tensor_scalar(
                out=gind[:], in0=g_, scalar1=st2[:, 3:4], scalar2=None,
                op0=ALU.is_ge), reads=(t_g, t_st2), writes=(t_gi,))
            dve(lambda e, g_=gsl: e.tensor_tensor(
                out=gwb[:], in0=gind[:], in1=g_, op=ALU.mult),
                reads=(t_gi, t_g), writes=(t_gwb,))
            tb = TBk[0:32, 0:128]
            tr(tb, gwb[:], self.identb[:],
               reads=(t_gwb, self.t_identb), writes=(tTB,))
            dve(lambda e, tb=tb, tt=tt: e.tensor_copy(
                gT[0:32, 128 * tt:128 * (tt + 1)], tb),
                reads=(tTB,), writes=(t_gT,))

        for ep in range(NE // 2):
            for e2 in range(2):
                e_ = 2 * ep + e2
                sl = e_ % 2
                dma("pool", kbf[:, 4096 * sl:4096 * (sl + 1)].rearrange(
                    "p (c f) -> p c f", c=8),
                    self.din["L0_ffn_keys"][e_].rearrange(
                        "(c p) f -> p c f", p=128),
                    writes=(t_kbf[sl],))
                dma("pool", vbf[:, 4096 * sl:4096 * (sl + 1)].rearrange(
                    "p (c f) -> p c f", c=4),
                    self.din["L0_ffn_values"][e_].rearrange(
                        "(c p) f -> p c f", p=128),
                    writes=(t_vbf[sl],))
                for seg in range(4):
                    pb, pbank = tP[5], P[5]
                    self.mm(pbank[:, 0:512],
                            sel32[0:32, 128 * e_:128 * (e_ + 1)],
                            gT[0:32, 512 * seg:512 * (seg + 1)], True, True,
                            reads=(t_sel32, t_gT), writes=(pb,))
                    act(lambda e, pk=pbank, sl=sl, seg=seg: e.activation(
                        out=gb[:, 2048 * sl + 512 * seg:2048 * sl + 512 * (seg + 1)],
                        in_=pk[:, 0:512], func=AF.Identity),
                        reads=(pb,), writes=(t_gb,))
                for fc in range(4):
                    for seg in range(4):
                        pb, pbank = tP[seg % 4], P[seg % 4]
                        for ci in range(8):
                            self.mm(pbank[:, 0:512],
                                    kbf[:, 4096 * sl + 512 * ci + 128 * fc:
                                        4096 * sl + 512 * ci + 128 * (fc + 1)],
                                    xn2T[:, 2048 * ci + 512 * seg:
                                         2048 * ci + 512 * (seg + 1)],
                                    ci == 0, ci == 7,
                                    reads=(t_kbf[sl], t_x2), writes=(pb,))
                        dve(lambda e, pk=pbank, sl=sl, fc=fc, seg=seg:
                            e.scalar_tensor_tensor(
                                out=spr[:, 8192 * sl + 2048 * fc + 512 * seg:
                                        8192 * sl + 2048 * fc + 512 * (seg + 1)],
                                in0=pk[:, 0:512], scalar=0.0,
                                in1=gb[:, 2048 * sl + 512 * seg:
                                       2048 * sl + 512 * (seg + 1)],
                                op0=ALU.max, op1=ALU.mult),
                            reads=(pb, t_gb), writes=(t_spr[sl],))
            for tt in range(NT):
                for dh in range(2):
                    pi = 4 + (2 * tt + dh) % 3
                    pb, pbank = tP[pi], P[pi]
                    first = True
                    for e2 in range(2):
                        for fc in range(4):
                            self.mm(pbank[:, 0:512],
                                    spr[:, 8192 * e2 + 2048 * fc + 128 * tt:
                                        8192 * e2 + 2048 * fc + 128 * (tt + 1)],
                                    vbf[:, 4096 * e2 + 1024 * fc + 512 * dh:
                                        4096 * e2 + 1024 * fc + 512 * (dh + 1)],
                                    first, e2 == 1 and fc == 3,
                                    reads=(t_spr[e2], t_vbf[e2]), writes=(pb,))
                            first = False
                    dve(lambda e, pk=pbank, tt=tt, dh=dh: e.tensor_tensor(
                        out=oac[:, 1024 * tt + 512 * dh:1024 * tt + 512 * (dh + 1)],
                        in0=oac[:, 1024 * tt + 512 * dh:1024 * tt + 512 * (dh + 1)],
                        in1=pk[:, 0:512], op=ALU.add),
                        reads=(pb, t_oac), writes=(t_oac,))
        dma("sync", self.dout[b].rearrange("(tt p) d -> p tt d", p=128),
            oac[:].rearrange("p (tt d) -> p tt d", tt=16),
            reads=(t_oac,), writes=(Trk(self.bar),))
        fx.close()


_CACHE = {}


def kernel(**inputs):
    consts = host_consts()
    mk = MK(stage="full")
    nc = mk.build()
    x = np.ascontiguousarray(inputs["L0_attn_x"])
    in_maps = []
    for cid in range(NCORES):
        m = {"L0_attn_x": x[2 * cid:2 * cid + 2]}
        for k, v in inputs.items():
            if k != "L0_attn_x":
                m[k] = np.ascontiguousarray(v)
        for k, v in consts.items():
            m["c_" + k] = v
        in_maps.append(m)
    res = run_bass_kernel_spmd(nc, in_maps, core_ids=list(range(NCORES)))
    out = np.concatenate([r["out"] for r in res.results], axis=0)
    return out.reshape(16, T, D)


# version 33
# speedup vs baseline: 1.1825x; 1.1825x over previous
import contextlib
import numpy as np
import ml_dtypes
import concourse.bass as bass
import concourse.mybir as mybir
from concourse.bass_utils import run_bass_kernel_spmd

F32 = mybir.dt.float32
BF16 = mybir.dt.bfloat16
FP16 = mybir.dt.float16
AF = mybir.ActivationFunctionType
ALU = mybir.AluOpType
AX = mybir.AxisListType

NB = 2
T = 2048
D = 1024
H = 16
NE = 32
EF = 512
NCORES = 8
NT = T // 128
PI = [0, 0, 0, 1, 1, 2]
PJ = [1, 2, 3, 2, 3, 3]
SGN = [1.0, -1.0, 1.0, 1.0, -1.0, 1.0]
CSHIFT = 40.0
SIMSAFE = True
DMASERIAL = True


class _Op:
    __slots__ = ("eng", "fn", "deps", "dma", "sig", "waits", "note")

    def __init__(self, eng, fn, deps, dma, note):
        self.eng, self.fn, self.deps, self.dma = eng, fn, deps, dma
        self.sig = None
        self.waits = ()
        self.note = note


class Trk:
    __slots__ = ("w", "rs")

    def __init__(self, init=None):
        self.w = init
        self.rs = []


class Sched:
    ENGS = ("pe", "dve", "act", "pool", "sync")

    def __init__(self):
        self.ops = {e: [] for e in self.ENGS}
        self.last = {e: None for e in self.ENGS}

    def add(self, eng, fn, reads=(), writes=(), dma=False, note=""):
        deps = []
        for t in reads:
            if t.w is not None:
                deps.append(t.w)
        for t in writes:
            if t.w is not None:
                deps.append(t.w)
            deps.extend(t.rs)
        op = _Op(eng, fn, deps, dma, note)
        self.ops[eng].append(op)
        self.last[eng] = op
        for t in reads:
            t.rs.append(op)
        for t in writes:
            t.w = op
            t.rs = []
        return op

    def barrier(self, note="bar"):
        deps = [o for o in self.last.values() if o is not None]
        op = _Op("pe", lambda e: e.nop(), deps, False, note)
        self.ops["pe"].append(op)
        self.last["pe"] = op
        return op

    def finalize(self):
        for eng in self.ENGS:
            for op in self.ops[eng]:
                for d in op.deps:
                    if d.dma or d.eng != op.eng or (SIMSAFE and d.eng != "pe"):
                        if d.sig is None:
                            d.sig = "need"
        for eng in self.ENGS:
            cc = dc = 0
            for op in self.ops[eng]:
                if op.dma:
                    dc += 16
                    op.sig = ("d", eng, dc)
                elif op.sig == "need":
                    cc += 1
                    op.sig = ("c", eng, cc)
                else:
                    op.sig = None
        for eng in self.ENGS:
            marks = {}
            for op in self.ops[eng]:
                ws = {}
                for d in op.deps:
                    if d.sig is None:
                        continue
                    if (not d.dma) and d.eng == eng and (
                            eng == "pe" or not SIMSAFE):
                        continue
                    kind, peng, val = d.sig
                    key = (kind, peng)
                    if val > marks.get(key, 0) and val > ws.get(key, 0):
                        ws[key] = val
                if DMASERIAL and op.dma and op.sig[2] > 16:
                    key = ("d", eng)
                    v = op.sig[2] - 16
                    if v > marks.get(key, 0) and v > ws.get(key, 0):
                        ws[key] = v
                for k, v in ws.items():
                    marks[k] = v
                op.waits = tuple(ws.items())

    def emit(self, nc, stack):
        self.finalize()
        sems = {}
        for eng in self.ENGS:
            if any((not o.dma) and o.sig for o in self.ops[eng]):
                sems[("c", eng)] = stack.enter_context(nc.semaphore(f"c_{eng}"))
            if any(o.dma for o in self.ops[eng]):
                sems[("d", eng)] = stack.enter_context(nc.semaphore(f"d_{eng}"))
        block = stack.enter_context(nc.Block())

        def body(eng):
            def run(engine):
                for op in self.ops[eng]:
                    for (key, val) in op.waits:
                        engine.wait_ge(sems[key], val)
                    inst = op.fn(engine)
                    if op.sig is not None:
                        kind, peng, val = op.sig
                        inst.then_inc(sems[(kind, peng)],
                                      16 if kind == "d" else 1)
                if eng == "sync":
                    last = 0
                    for op in self.ops[eng]:
                        if op.dma:
                            last = op.sig[2]
                    if last:
                        engine.wait_ge(sems[("d", "sync")], last)
            return run

        block.tensor(body("pe"))
        block.vector(body("dve"))
        block.scalar(body("act"))
        block.gpsimd(body("pool"))
        block.sync(body("sync"))


def host_consts():
    i_ = np.arange(128)[:, None]
    j_ = np.arange(512)[None, :]
    diffc = np.zeros((128, 2048), np.float16)
    mc = np.zeros((128, 2048), ml_dtypes.bfloat16)
    m2c = np.zeros((128, 2048), ml_dtypes.bfloat16)
    for c in range(4):
        d = (j_ - i_ - 128 * c).astype(np.float32)
        diffc[:, 512 * c:512 * (c + 1)] = d.astype(np.float16)
        mc[:, 512 * c:512 * (c + 1)] = (d > 0).astype(ml_dtypes.bfloat16)
        m2c[:, 512 * c:512 * (c + 1)] = (d >= 0).astype(ml_dtypes.bfloat16)
    eqpat = np.tile((np.arange(2048) % 512).astype(np.float16)[None, :], (128, 1))
    kneg = np.tile((-np.arange(1536)).astype(np.float16)[None, :], (128, 1))
    sel32 = np.zeros((32, 32 * 128), ml_dtypes.bfloat16)
    for e in range(NE):
        sel32[e, 128 * e:128 * (e + 1)] = 1.0
    return {
        "identb": np.eye(128, dtype=np.float32).astype(ml_dtypes.bfloat16),
        "diffc": diffc, "mc": mc, "m2c": m2c,
        "eqpat": eqpat, "kneg": kneg, "sel32": sel32,
    }


NPDT = {np.dtype(np.float32): F32, np.dtype(np.float16): FP16,
        np.dtype(ml_dtypes.bfloat16): BF16}


class MK:
    def __init__(self, stage="full", dbg=()):
        self.stage = stage
        self.dbg = list(dbg)
        self.nc = bass.Bass("TRN2", target_bir_lowering=False)
        self.S = Sched()
        self.ex = contextlib.ExitStack()
        self.dbg_out = {}
        self.bar = None

    def trk(self):
        return Trk(self.bar)

    def sb(self, stack, name, shape, dt):
        t = stack.enter_context(self.nc.sbuf_tensor(name, shape, dt))
        return t, self.trk()

    def pe(self, fn, reads=(), writes=(), note=""):
        return self.S.add("pe", fn, reads, writes, note=note)

    def dve(self, fn, reads=(), writes=(), note=""):
        return self.S.add("dve", fn, reads, writes, note=note)

    def act(self, fn, reads=(), writes=(), note=""):
        return self.S.add("act", fn, reads, writes, note=note)

    def pool(self, fn, reads=(), writes=(), note=""):
        return self.S.add("pool", fn, reads, writes, note=note)

    def dma(self, queue, out, in_, reads=(), writes=(), note=""):
        return self.S.add(queue,
                          lambda e, o=out, i=in_: e.dma_start(out=o, in_=i),
                          reads, writes, dma=True, note=note)

    def mm(self, out, lhsT, rhs, start, stop, reads=(), writes=(), note="",
           tp=None):
        def f(e, o=out, l=lhsT, r=rhs, st=start, sp=stop, t=tp):
            return e.matmul(o, l, r, start=st, stop=sp, tile_position=t)
        return self.pe(f, reads, writes, note=note)

    def tr(self, out, in_, ident, reads=(), writes=(), note=""):
        def f(e, o=out, i=in_, d=ident):
            return e.transpose(o, i, d)
        return self.pe(f, reads, writes, note=note)

    def fill(self, reads=(), bank=0, n=512):
        self.mm(self.P[bank][0:1, 0:n], self.onesb[0:1, 0:1],
                self.onesb[0:1, 0:n], True, True,
                reads=(self.t_onesb,) + tuple(reads),
                writes=(self.tP[bank],))

    def dbg_dump(self, name, b, ap, trk):
        if (name, b) in self.dbg:
            self.dbg.remove((name, b))
            key = f"{name}_b{b}"
            d = self.nc.dram_tensor("dbg_" + key, list(ap.shape),
                                    ap.dtype, kind="ExternalOutput").ap()
            self.dma("sync", d, ap, reads=(trk,), note="dbg" + key)
            self.dbg_out[key] = True

    def build(self):
        nc = self.nc
        ex = self.ex
        self.din = {}
        specs = [("L0_attn_x", [NB, T, D]), ("L0_attn_w_qkv", [3 * D, D]),
                 ("L0_attn_b_qkv", [3 * D]), ("L0_attn_w1_write", [64, D]),
                 ("L0_attn_w2_write", [64, D]), ("L0_attn_w1_read", [64, D]),
                 ("L0_attn_w2_read", [64, D]), ("L0_attn_w_out", [D, D]),
                 ("L0_attn_b_out", [D]), ("L0_attn_decay_logits", [H]),
                 ("L0_attn_bias_scale", [H]), ("L0_ffn_expert_sel", [NE, D]),
                 ("L0_ffn_keys", [NE, D, EF]), ("L0_ffn_values", [NE, EF, D])]
        for nm, shp in specs:
            self.din[nm] = nc.dram_tensor(nm, shp, F32, kind="ExternalInput").ap()
        for nm, arr in host_consts().items():
            self.din["c_" + nm] = nc.dram_tensor(
                "c_" + nm, list(arr.shape), NPDT[arr.dtype],
                kind="ExternalInput").ap()
        self.dout = nc.dram_tensor("out", [NB, T, D], F32,
                                   kind="ExternalOutput").ap()
        self.dscr = nc.dram_tensor("xprime", [NB, T, D], F32,
                                   kind="Internal").ap()
        self.t_dscr = Trk()

        sb = lambda n, s, d: self.sb(ex, n, s, d)
        self.identb, self.t_identb = sb("identb", [128, 128], BF16)
        self.ones, self.t_ones = sb("ones", [128, 512], F32)
        self.onesb, self.t_onesb = sb("onesb", [128, 512], BF16)
        self.borow, self.t_borow = sb("borow", [1, D], BF16)
        self.drow, self.t_drow = sb("drow", [1, H], F32)
        self.ldrow, self.t_ldrow = sb("ldrow", [1, H], F32)
        self.bsrow, self.t_bsrow = sb("bsrow", [1, H], F32)

        self.P2 = [ex.enter_context(nc.psum_tensor(f"pp{i}", [128, 1024], F32))
                   for i in range(4)]
        self.P = [self.P2[i // 2][:, 512 * (i % 2):512 * (i % 2 + 1)]
                  for i in range(8)]
        self.TBk = self.P2[3][:].bitcast(BF16)
        self.tP = [self.trk() for _ in range(8)]
        self.tTB = self.tP[6]

        self.load_consts()
        for b in range(NB):
            self.batch(b)
            if self.stage in ("lines", "qkv"):
                break
        self.S.emit(nc, ex)
        self.ex.close()
        return nc

    def load_consts(self):
        d = self.dma
        d("sync", self.identb[:], self.din["c_identb"], writes=(self.t_identb,))
        r1 = lambda ap: ap.rearrange("(a n) -> a n", a=1)
        d("pool", self.borow[:], r1(self.din["L0_attn_b_out"]), writes=(self.t_borow,))
        d("sync", self.drow[:], r1(self.din["L0_attn_decay_logits"]),
          writes=(self.t_drow,))
        d("sync", self.bsrow[:], r1(self.din["L0_attn_bias_scale"]),
          writes=(self.t_bsrow,))
        self.dve(lambda e: e.memset(self.ones[:], 1.0), writes=(self.t_ones,))
        self.dve(lambda e: e.tensor_copy(self.onesb[:], self.ones[:]),
                 reads=(self.t_ones,), writes=(self.t_onesb,))
        self.act(lambda e: e.activation(out=self.drow[:], in_=self.drow[:],
                 func=AF.Sigmoid), reads=(self.t_drow,), writes=(self.t_drow,))
        self.act(lambda e: e.activation(out=self.ldrow[:], in_=self.drow[:],
                 func=AF.Ln), reads=(self.t_drow,), writes=(self.t_ldrow,))

    def batch(self, b):
        ax = contextlib.ExitStack()
        self.avt, self.t_avt = self.sb(ax, f"avt{b}", [128, 8 * 2048], BF16)
        self.attn(b, ax)
        if self.stage == "full":
            self.wout(b)
        ax.close()
        self.bar = self.S.barrier(f"bar_attn{b}")
        if self.stage == "full":
            self.ffn(b)
        self.bar = self.S.barrier(f"bar_b{b}")

    def attn(self, b, ax):
        sb = lambda n, s, d: self.sb(ax, f"{n}{b}", s, d)
        self.diffc, self.t_diffc = sb("diffc", [128, 2048], FP16)
        self.mc, self.t_mc = sb("mc", [128, 2048], BF16)
        self.m2c, self.t_m2c = sb("m2c", [128, 2048], BF16)
        self.eqpat, self.t_eqpat = sb("eqpat", [128, 2048], FP16)
        self.kneg, self.t_kneg = sb("kneg", [128, 1536], FP16)
        self.brow, self.t_brow = sb("brow", [1, 3 * D], BF16)
        for nm, t_, trk in (("c_diffc", self.diffc, self.t_diffc),
                            ("c_mc", self.mc, self.t_mc),
                            ("c_m2c", self.m2c, self.t_m2c),
                            ("c_eqpat", self.eqpat, self.t_eqpat),
                            ("c_kneg", self.kneg, self.t_kneg)):
            self.dma("sync", t_[:], self.din[nm], writes=(trk,))
        self.dma("pool", self.brow[:],
                 self.din["L0_attn_b_qkv"].rearrange("(a n) -> a n", a=1),
                 writes=(self.t_brow,))
        S, nc = self.S, self.nc
        pe, dve, act, dma, mm, tr = (self.pe, self.dve, self.act, self.dma,
                                     self.mm, self.tr)
        P, tP, TBk, tTB = self.P, self.tP, self.TBk, self.tTB

        xnT, t_xnT = sb("xnT", [128, 8 * 2052], BF16)
        wqT, _ = sb("wqT", [128, 2 * 8 * 384], BF16)
        t_wq = [self.trk() for _ in range(16)]
        w4T, t_w4 = sb("w4T", [128, 8 * 256], BF16)
        linw, t_lw = sb("linw", [128, NT * 96], BF16)
        linr, t_lr = sb("linr", [128, NT * 96], BF16)
        lst, t_lst = sb("lst", [128, 8 * 96], F32)
        l2s, t_l2s = sb("l2s", [128, 8 * 96], F32)
        nst, t_nst = sb("nst", [128, 8 * 16], F32)
        tm1, t_tm1 = sb("tm1", [128, 8 * 16], F32)
        qt_, t_qt = sb("qt", [128, 2048], BF16)
        kt_, t_kt = sb("kt", [128, 2048], BF16)
        vp_, t_vp = sb("vp", [128, NT * 132], BF16)
        rtg, t_rtg = sb("rtg", [128, 2048], BF16)
        jwb, t_jwb = sb("jwb", [128, 2048], BF16)
        rpp, t_rpp = sb("rpp", [128, 2048], BF16)
        jpp, t_jpp = sb("jpp", [128, 3 * 1536], BF16)
        ebuf, t_eb = sb("ebuf", [128, 1536], BF16)
        wdg, t_wdg = sb("wdg", [128, 8 * 512], BF16)
        trs, t_trs = sb("trs", [32, 2048], BF16)
        negc, t_negc = sb("negc", [128, 2 * 512], BF16)
        les, t_les = sb("les", [128, 32], F32)
        nc4, t_nc4 = sb("nc4", [128, 8], BF16)
        ptb, t_ptb = sb("ptb", [128, 3 * 1024], BF16)
        dum, t_dum = sb("dum", [128, 512], BF16)
        tdg, t_tdg = sb("tdg", [128, 2 * 512], BF16)
        avu, t_avu = sb("avu", [128, 1024], BF16)
        lin, t_lin = sb("lin", [128, 1024], BF16)
        xt_, t_xt = sb("xt", [128, 2 * 1024], F32)
        extb, t_ext = xt_, t_xt
        xnstg, t_xns = sb("xnstg", [128, 1024], BF16)
        wld, t_wld = sb("wld", [128, 4 * 1024], BF16)
        t_wldS = [self.trk() for _ in range(4)]
        self._wld_pref = False
        ldg, t_ldg = sb("ldg", [128, 1], F32)
        bsg, t_bsg = sb("bsg", [128, 1], F32)
        lqb, t_lqb = sb("lqb", [128, 1], F32)
        st1, t_st1 = sb("st1", [128, 16], F32)
        ldbc, t_ldbc = sb("ldbc", [128, H], F32)
        dx = self.din["L0_attn_x"]

        mm(P[6][:, 0:H], self.ones[0:1, 0:128], self.ldrow[:], True, True,
           reads=(self.t_ones, self.t_ldrow), writes=(tP[6],))
        dve(lambda e: e.tensor_copy(ldbc[:], P[6][:, 0:H]),
            reads=(tP[6],), writes=(t_ldbc,))

        for c in range(8):
            dve(lambda e, c=c: e.memset(xnT[:, 2052 * c:2052 * c + 1], 0.0),
                writes=(t_xnT,))
            dve(lambda e, c=c: e.memset(xnT[:, 2052 * c + 2049:2052 * (c + 1)],
                0.0), writes=(t_xnT,))
        for tt in range(NT):
            xsl = xt_[:, 1024 * (tt % 2):1024 * (tt % 2 + 1)]
            dma("sync", xsl, dx[b, 128 * tt:128 * (tt + 1), :], writes=(t_xt,))
            dve(lambda e, xs=xsl: e.tensor_reduce(
                out=st1[:, 0:1], in_=xs, axis=AX.X, op=ALU.add),
                reads=(t_xt,), writes=(t_st1,))
            dve(lambda e: e.tensor_scalar(out=st1[:, 0:1], in0=st1[:, 0:1],
                scalar1=1.0 / D, scalar2=None, op0=ALU.mult),
                reads=(t_st1,), writes=(t_st1,))
            act(lambda e, xs=xsl: e.activation(
                out=xnstg[:], in_=xs, func=AF.Square, accum_out=st1[:, 1:2]),
                reads=(t_xt,), writes=(t_xns, t_st1))
            dve(lambda e: e.tensor_scalar(out=st1[:, 1:2], in0=st1[:, 1:2],
                scalar1=1.0 / D, scalar2=None, op0=ALU.mult),
                reads=(t_st1,), writes=(t_st1,))
            dve(lambda e: e.tensor_tensor(out=st1[:, 2:3], in0=st1[:, 0:1],
                in1=st1[:, 0:1], op=ALU.mult), reads=(t_st1,), writes=(t_st1,))
            dve(lambda e: e.tensor_tensor(out=st1[:, 1:2], in0=st1[:, 1:2],
                in1=st1[:, 2:3], op=ALU.subtract),
                reads=(t_st1,), writes=(t_st1,))
            dve(lambda e: e.tensor_scalar(out=st1[:, 1:2], in0=st1[:, 1:2],
                scalar1=1e-5, scalar2=None, op0=ALU.add),
                reads=(t_st1,), writes=(t_st1,))
            act(lambda e: e.activation(out=st1[:, 1:2], in_=st1[:, 1:2],
                func=AF.Ln), reads=(t_st1,), writes=(t_st1,))
            act(lambda e: e.activation(out=st1[:, 1:2], in_=st1[:, 1:2],
                func=AF.Exp, scale=-0.5), reads=(t_st1,), writes=(t_st1,))
            dve(lambda e, xs=xsl: e.tensor_scalar(
                out=xnstg[:], in0=xs, scalar1=st1[:, 0:1], scalar2=st1[:, 1:2],
                op0=ALU.subtract, op1=ALU.mult),
                reads=(t_xt, t_st1), writes=(t_xns,))
            for c in range(8):
                tb = TBk[:, 128 * (c % 8):128 * (c % 8 + 1)]
                tr(tb, xnstg[:, 128 * c:128 * (c + 1)], self.identb[:],
                   reads=(t_xns, self.t_identb), writes=(tTB,))
                dve(lambda e, tb=tb, c=c, tt=tt: e.tensor_copy(
                    xnT[:, 2052 * c + 1 + 128 * tt:2052 * c + 1 + 128 * (tt + 1)],
                    tb), reads=(tTB,), writes=(t_xnT,))

        dve(lambda e: e.memset(wld[:, 0:1024], 0.0), writes=(t_wld,))
        for wi, nm in enumerate(("L0_attn_w1_write", "L0_attn_w2_write",
                                 "L0_attn_w1_read", "L0_attn_w2_read")):
            dma("pool", wld[0:64, 0:1024], self.din[nm], writes=(t_wld,))
            for c in range(8):
                tb = TBk[:, 128 * (c % 8):128 * (c % 8 + 1)]
                tr(tb, wld[:, 128 * c:128 * (c + 1)], self.identb[:],
                   reads=(t_wld, self.t_identb), writes=(tTB,))
                dve(lambda e, tb=tb, c=c, wi=wi: e.tensor_copy(
                    w4T[:, 256 * c + 64 * wi:256 * c + 64 * (wi + 1)],
                    tb[:, 0:64]), reads=(tTB,), writes=(t_w4,))

        ev4 = extb[:].rearrange("p (t w h q) -> p t w h q", w=4, h=16, q=4)
        lwv = linw[:].rearrange("p (t h c) -> p t h c", h=16, c=6)
        lrv = linr[:].rearrange("p (t h c) -> p t h c", h=16, c=6)
        lsv = lst[:].rearrange("p (t h c) -> p t h c", h=16, c=6)
        tmv = tm1[:].rearrange("p (t h) -> p t h", h=16)
        for ch in range(2):
            for i8 in range(8):
                tt = 8 * ch + i8
                pb, pbank = tP[tt % 2], P[tt % 2]
                for c in range(8):
                    self.mm(pbank[:, 0:64],
                            xnT[:, 2052 * c + 128 * tt:2052 * c + 128 * (tt + 1)],
                            w4T[:, 256 * c:256 * c + 64],
                            c == 0, c == 7, reads=(t_xnT, t_w4), writes=(pb,))
                for c in range(8):
                    self.mm(pbank[:, 64:256],
                            xnT[:, 2052 * c + 1 + 128 * tt:
                                2052 * c + 1 + 128 * (tt + 1)],
                            w4T[:, 256 * c + 64:256 * c + 256],
                            c == 0, c == 7, reads=(t_xnT, t_w4), writes=(pb,))
                dve(lambda e, i8=i8, pk=pbank: e.tensor_copy(
                    extb[:, 256 * i8:256 * (i8 + 1)], pk[:, 0:256]),
                    reads=(pb,), writes=(t_ext,))
            for (lv, t_lo, b1, b2, fold) in ((lwv, t_lw, 0, 1, False),
                                             (lrv, t_lr, 2, 3, True)):
                for cpos in range(6):
                    jj = (5 - cpos) if fold else cpos
                    swap = fold and SGN[cpos] < 0
                    pi, pj = PI[jj], PJ[jj]
                    dve(lambda e, i0=ev4[:, :, b1, :, pi], i1=ev4[:, :, b2, :, pj]:
                        e.tensor_tensor(out=tmv, in0=i0, in1=i1, op=ALU.mult),
                        reads=(t_ext,), writes=(t_tm1,))
                    dve(lambda e, o=lsv[:, :, :, cpos], i0=ev4[:, :, b1, :, pj],
                        i1=ev4[:, :, b2, :, pi]:
                        e.tensor_tensor(out=o, in0=i0, in1=i1, op=ALU.mult),
                        reads=(t_ext,), writes=(t_lst,))
                    if swap:
                        dve(lambda e, o=lsv[:, :, :, cpos], i0=lsv[:, :, :, cpos]:
                            e.tensor_tensor(out=o, in0=i0, in1=tmv,
                                            op=ALU.subtract),
                            reads=(t_lst, t_tm1), writes=(t_lst,))
                    else:
                        dve(lambda e, o=lsv[:, :, :, cpos], i1=lsv[:, :, :, cpos]:
                            e.tensor_tensor(out=o, in0=tmv, in1=i1,
                                            op=ALU.subtract),
                            reads=(t_lst, t_tm1), writes=(t_lst,))
                dve(lambda e: e.tensor_tensor(out=l2s[:], in0=lst[:], in1=lst[:],
                    op=ALU.mult), reads=(t_lst,), writes=(t_l2s,))
                dve(lambda e: e.tensor_reduce(
                    out=nst[:].rearrange("p (t h o) -> p t h o", h=16, o=1),
                    in_=l2s[:].rearrange("p (t h c) -> p t h c", h=16, c=6),
                    axis=AX.X, op=ALU.add), reads=(t_l2s,), writes=(t_nst,))
                dve(lambda e: e.tensor_scalar(out=nst[:], in0=nst[:],
                    scalar1=1e-24, scalar2=None, op0=ALU.max),
                    reads=(t_nst,), writes=(t_nst,))
                act(lambda e: e.activation(out=nst[:], in_=nst[:], func=AF.Ln),
                    reads=(t_nst,), writes=(t_nst,))
                act(lambda e: e.activation(out=nst[:], in_=nst[:], func=AF.Exp,
                    scale=-0.5), reads=(t_nst,), writes=(t_nst,))
                dve(lambda e, o=lv[:, 8 * ch:8 * ch + 8, :, :]:
                    e.tensor_tensor(
                        out=o, in0=lsv[:, :, :, :],
                        in1=nst[:].rearrange("p (t h o) -> p t h o", h=16, o=1
                                             ).broadcast_to([128, 8, 16, 6]),
                        op=ALU.mult), reads=(t_lst, t_nst), writes=(t_lo,))

        self.dbg_dump("xnT", b, xnT[:], t_xnT)
        self.dbg_dump("linw", b, linw[:], t_lw)
        self.dbg_dump("linr", b, linr[:], t_lr)
        if self.stage == "lines":
            return

        ctx = dict(b=b, xnT=xnT, t_xnT=t_xnT, wqT=wqT, t_wq=t_wq,
                   wld=wld, t_wld=t_wld, t_wldS=t_wldS,
                   linw=linw, t_lw=t_lw, linr=linr, t_lr=t_lr,
                   qt_=qt_, t_qt=t_qt, kt_=kt_, t_kt=t_kt, vp_=vp_, t_vp=t_vp,
                   rtg=rtg, t_rtg=t_rtg, jwb=jwb, t_jwb=t_jwb,
                   rpp=rpp, t_rpp=t_rpp, jpp=jpp, t_jpp=t_jpp,
                   ebuf=ebuf, t_eb=t_eb, wdg=wdg, t_wdg=t_wdg,
                   trs=trs, t_trs=t_trs, negc=negc, t_negc=t_negc,
                   les=les, t_les=t_les, nc4=nc4, t_nc4=t_nc4,
                   ptb=ptb, t_ptb=t_ptb, t_ptbS=[self.trk() for _ in range(3)],
                   t_rppB=[self.trk() for _ in range(4)],
                   dum=dum, t_dum=t_dum,
                   tdg=tdg, t_tdg=t_tdg, avu=avu, t_avu=t_avu,
                   lin=lin, t_lin=t_lin, ldg=ldg, t_ldg=t_ldg,
                   bsg=bsg, t_bsg=t_bsg, lqb=lqb, t_lqb=t_lqb,
                   st1=st1, t_st1=t_st1, ldbc=ldbc, t_ldbc=t_ldbc)
        for g in range(4):
            self.group(g, ctx)
        self.dbg_dump("avt", b, self.avt[:], self.t_avt)
        self._wbufs = ((qt_, t_qt), (kt_, t_kt), (rtg, t_rtg), (jwb, t_jwb),
                       (wld, t_wld), (xt_, t_xt))

    def group(self, g, c):
        S, nc = self.S, self.nc
        pe, dve, act, dma, mm, tr = (self.pe, self.dve, self.act, self.dma,
                                     self.mm, self.tr)
        P, tP, TBk, tTB = self.P, self.tP, self.TBk, self.tTB
        b = c["b"]

        dve(lambda e: e.memset(c["ldg"][:], 0.0), writes=(c["t_ldg"],))
        dve(lambda e: e.memset(c["bsg"][:], 1.0), writes=(c["t_bsg"],))
        for h2 in range(4):
            dma("sync", c["ldg"][32 * h2:32 * h2 + 6, 0:1],
                self.ldrow[0:1, 4 * g + h2:4 * g + h2 + 1].broadcast_to([1, 6]),
                reads=(self.t_ldrow,), writes=(c["t_ldg"],))
            dma("sync", c["bsg"][32 * h2:32 * h2 + 6, 0:1],
                self.bsrow[0:1, 4 * g + h2:4 * g + h2 + 1].broadcast_to([1, 6]),
                reads=(self.t_bsrow,), writes=(c["t_bsg"],))

        for (src, t_src, dstt, t_dst) in (
                (c["linw"], c["t_lw"], c["jwb"], c["t_jwb"]),
                (c["linr"], c["t_lr"], c["rtg"], c["t_rtg"])):
            dve(lambda e: e.memset(c["trs"][0:32, :], 1.0),
                writes=(c["t_trs"],))
            for tt in range(NT):
                tb24 = TBk[0:24, 128 * (tt % 8):128 * (tt % 8 + 1)]
                tr(tb24, src[:, 96 * tt + 24 * g:96 * tt + 24 * (g + 1)],
                   self.identb[:], reads=(t_src, self.t_identb),
                   writes=(tTB,))
                dve(lambda e, tb24=tb24, tt=tt: e.tensor_copy(
                    c["trs"][0:24, 128 * tt:128 * (tt + 1)],
                    tb24), reads=(tTB,), writes=(c["t_trs"],))
            for h2 in range(4):
                dma("sync", dstt[32 * h2:32 * h2 + 6, :],
                    c["trs"][6 * h2:6 * h2 + 6, :],
                    reads=(c["t_trs"],), writes=(t_dst,))
            dma("sync", dstt[6:7, 0:2048] if False else dstt[0:0, 0:0],
                c["trs"][0:0, :]) if False else None
            for h2 in range(4):
                dma("sync", dstt[32 * h2 + 6:32 * h2 + 7, :],
                    c["trs"][24:25, :],
                    reads=(c["t_trs"],), writes=(t_dst,))
        dve(lambda e: e.tensor_scalar(out=c["jwb"][:], in0=c["jwb"][:],
            scalar1=c["bsg"][:, 0:1], scalar2=None, op0=ALU.mult),
            reads=(c["t_jwb"], c["t_bsg"]), writes=(c["t_jwb"],))
        act(lambda e: e.activation(out=c["ebuf"][:, 0:1536], in_=self.eqpat[:, 0:1536],
            func=AF.Exp, scale=c["ldg"][:, 0:1], bias=0.0),
            reads=(self.t_eqpat, c["t_ldg"]), writes=(c["t_eb"],))
        dve(lambda e: e.tensor_tensor(out=c["rpp"][:, 0:1536], in0=c["rtg"][:, 0:1536],
            in1=c["ebuf"][:, 0:1536], op=ALU.mult),
            reads=(c["t_rtg"], c["t_eb"]),
            writes=(c["t_rppB"][0], c["t_rppB"][1], c["t_rppB"][2]))
        act(lambda e: e.activation(out=c["ebuf"][:, 0:512],
            in_=self.eqpat[:, 1536:2048],
            func=AF.Exp, scale=c["ldg"][:, 0:1], bias=0.0),
            reads=(self.t_eqpat, c["t_ldg"]), writes=(c["t_eb"],))
        dve(lambda e: e.tensor_tensor(out=c["rpp"][:, 1536:2048],
            in0=c["rtg"][:, 1536:2048], in1=c["ebuf"][:, 0:512], op=ALU.mult),
            reads=(c["t_rtg"], c["t_eb"]), writes=(c["t_rppB"][3],))
        for qb in (1, 2, 3):
            n = 512 * qb
            dve(lambda e, qb=qb: e.tensor_scalar(out=c["lqb"][:], in0=c["ldg"][:],
                scalar1=512.0 * qb, scalar2=None, op0=ALU.mult),
                reads=(c["t_ldg"],), writes=(c["t_lqb"],))
            act(lambda e, n=n: e.activation(out=c["ebuf"][:, 0:n],
                in_=self.kneg[:, 0:n], func=AF.Exp,
                scale=c["ldg"][:, 0:1], bias=c["lqb"][:, 0:1]),
                reads=(self.t_kneg, c["t_ldg"], c["t_lqb"]), writes=(c["t_eb"],))
            dve(lambda e, qb=qb, n=n: e.tensor_tensor(
                out=c["jpp"][:, 1536 * (qb - 1):1536 * (qb - 1) + n],
                in0=c["jwb"][:, 0:n], in1=c["ebuf"][:, 0:n], op=ALU.mult),
                reads=(c["t_jwb"], c["t_eb"]), writes=(c["t_jpp"],))

        self.dbg_dump(f"rtg{g}", b, c["rtg"][:], c["t_rtg"])
        self.dbg_dump(f"jwb{g}", b, c["jwb"][:], c["t_jwb"])
        self.dbg_dump(f"rpp{g}", b, c["rpp"][:], c["t_rppB"][3])
        self.dbg_dump(f"jpp{g}", b, c["jpp"][:], c["t_jpp"])

        for hp in range(2):
            self.headpair(g, hp, c)

    def headpair(self, g, hp, c):
        pe, dve, act, dma, mm, tr = (self.pe, self.dve, self.act, self.dma,
                                     self.mm, self.tr)
        P, tP, TBk, tTB = self.P, self.tP, self.TBk, self.tTB
        b = c["b"]
        hpg = 2 * g + hp
        xnT, t_xnT = c["xnT"], c["t_xnT"]
        wqT, t_wq = c["wqT"], c["t_wq"]
        qt_, kt_, vp_ = c["qt_"], c["kt_"], c["vp_"]
        t_qt, t_kt, t_vp = c["t_qt"], c["t_kt"], c["t_vp"]

        wld, t_wld = c["wld"], c["t_wld"]
        t_wldS = c["t_wldS"]
        slot = 3072 * (hpg % 2)
        wsl = lambda k, wi: (3 * k + wi) % 4
        if not self._wld_pref:
            for wi, rbase in enumerate((128 * hpg, D + 128 * hpg,
                                        2 * D + 128 * hpg)):
                s4 = wsl(hpg, wi)
                dma("pool", wld[:, 1024 * s4:1024 * (s4 + 1)],
                    self.din["L0_attn_w_qkv"][rbase:rbase + 128, :],
                    writes=(t_wldS[s4], t_wld))
            self._wld_pref = True
        for wi in range(3):
            s4 = wsl(hpg, wi)
            dst = wld[:, 1024 * s4:1024 * (s4 + 1)]
            for ci in range(8):
                tb = TBk[:, 128 * (ci % 8):128 * (ci % 8 + 1)]
                tr(tb, dst[:, 128 * ci:128 * (ci + 1)], self.identb[:],
                   reads=(t_wldS[s4], t_wld, self.t_identb), writes=(tTB,))
                dve(lambda e, tb=tb, ci=ci, wi=wi: e.tensor_copy(
                    wqT[:, slot + 384 * ci + 128 * wi:slot + 384 * ci + 128 * (wi + 1)],
                    tb), reads=(tTB,), writes=(t_wq[(hpg % 2) * 8 + ci],))
        if hpg < 7:
            for wi, rbase in enumerate((128 * (hpg + 1), D + 128 * (hpg + 1),
                                        2 * D + 128 * (hpg + 1))):
                s4 = wsl(hpg + 1, wi)
                dma("pool", wld[:, 1024 * s4:1024 * (s4 + 1)],
                    self.din["L0_attn_w_qkv"][rbase:rbase + 128, :],
                    writes=(t_wldS[s4],))

        for (wi, base_row, dst, t_dst, scale) in (
                (0, 128 * hpg, qt_, t_qt, 0.125),
                (1, D + 128 * hpg, kt_, t_kt, 1.0)):
            for seg in range(4):
                pb, pbank = tP[seg % 2], P[seg % 2]
                self.mm(pbank[:, 0:512],
                        self.brow[0:1, base_row:base_row + 128],
                        self.onesb[0:1, 0:512], True, False,
                        reads=(self.t_brow, self.t_onesb), writes=(pb,))
                for ci in range(8):
                    self.mm(pbank[:, 0:512],
                            wqT[:, slot + 384 * ci + 128 * wi:
                                slot + 384 * ci + 128 * (wi + 1)],
                            xnT[:, 2052 * ci + 1 + 512 * seg:
                                2052 * ci + 1 + 512 * (seg + 1)],
                            False, ci == 7,
                            reads=(t_wq[(hpg % 2) * 8 + ci], t_xnT),
                            writes=(pb,))
                act(lambda e, pk=pbank, dst=dst, seg=seg, sc=scale: e.activation(
                    out=dst[:, 512 * seg:512 * (seg + 1)], in_=pk[:, 0:512],
                    func=AF.Identity, scale=sc),
                    reads=(pb,), writes=(t_dst,))
        vstg = c["ptb"]
        t_vstg = tuple(c["t_ptbS"]) + (c["t_ptb"],)
        base_row = 2 * D + 128 * hpg
        for seg in range(4):
            pb, pbank = tP[seg % 2], P[seg % 2]
            self.mm(pbank[:, 0:512],
                    self.brow[0:1, base_row:base_row + 128],
                    self.onesb[0:1, 0:512], True, False,
                    reads=(self.t_brow, self.t_onesb), writes=(pb,))
            for ci in range(8):
                self.mm(pbank[:, 0:512],
                        wqT[:, slot + 384 * ci + 256:slot + 384 * ci + 384],
                        xnT[:, 2052 * ci + 1 + 512 * seg:
                            2052 * ci + 1 + 512 * (seg + 1)],
                        False, ci == 7,
                        reads=(t_wq[(hpg % 2) * 8 + ci], t_xnT),
                        writes=(pb,))
            act(lambda e, pk=pbank, seg=seg: e.activation(
                out=vstg[:, 512 * seg:512 * (seg + 1)], in_=pk[:, 0:512],
                func=AF.Identity), reads=(pb,), writes=t_vstg)
        for tt in range(NT):
            dve(lambda e, tt=tt: e.memset(
                vp_[:, 132 * tt + 64:132 * tt + 65], 1.0), writes=(t_vp,))
            dve(lambda e, tt=tt: e.memset(
                vp_[:, 132 * tt + 130:132 * tt + 131], 1.0), writes=(t_vp,))
            tb = TBk[:, 128 * (tt % 8):128 * (tt % 8 + 1)]
            tr(tb, vstg[:, 128 * tt:128 * (tt + 1)], self.identb[:],
               reads=t_vstg + (self.t_identb,), writes=(tTB,))
            dve(lambda e, tb=tb, tt=tt: e.tensor_copy(
                vp_[:, 132 * tt + 0:132 * tt + 64], tb[:, 0:64]),
                reads=(tTB,), writes=(t_vp,))
            dve(lambda e, tb=tb, tt=tt: e.tensor_copy(
                vp_[:, 132 * tt + 66:132 * tt + 130], tb[:, 64:128]),
                reads=(tTB,), writes=(t_vp,))

        if self.stage == "qkv" and hpg == 0:
            self.dbg_dump("qt", b, qt_[:], t_qt)
            self.dbg_dump("kt", b, kt_[:], t_kt)
            self.dbg_dump("vp", b, vp_[:], t_vp)
        if self.stage == "qkv":
            return

        for h2 in range(2):
            h = 4 * g + 2 * hp + h2
            for cc in range(4):
                sl = c["wdg"][:, 1024 * cc + 512 * h2:1024 * cc + 512 * (h2 + 1)]
                act(lambda e, sl=sl, cc=cc, h=h: e.activation(
                    out=sl, in_=self.diffc[:, 512 * cc:512 * (cc + 1)],
                    func=AF.Exp, scale=c["ldbc"][:, h:h + 1]),
                    reads=(self.t_diffc, c["t_ldbc"]), writes=(c["t_wdg"],))
                dve(lambda e, sl=sl, cc=cc: e.tensor_tensor(
                    out=sl, in0=sl, in1=self.mc[:, 512 * cc:512 * (cc + 1)],
                    op=ALU.mult),
                    reads=(c["t_wdg"], self.t_mc), writes=(c["t_wdg"],))

        c["rot"] = 0
        st0 = self.stats_steps(g, hp, 0, c)
        for s in st0:
            s()
        for qb in range(4):
            extra = self.stats_steps(g, hp, qb + 1, c) if qb < 3 else []
            self.kloop(g, hp, qb, c, extra)

    def stats_steps(self, g, hp, qb, c):
        dve, act, dma = self.dve, self.act, self.dma
        P, P2, tP = self.P, self.P2, self.tP
        t_qt, t_kt = c["t_qt"], c["t_kt"]
        SL = (0, 1, 3)
        steps = []
        for qt in range(4 * qb, 4 * qb + 4):
            for kb in range(qb + 1):
                def s_emit(qt=qt, kb=kb):
                    sl = SL[c["rot"] % 3]
                    c["rot"] += 1
                    trks = (tP[2 * sl], tP[2 * sl + 1])
                    for h2 in range(2):
                        hrow = 64 * h2
                        self.mm(P[2 * sl + h2][:, 0:512],
                                c["qt_"][hrow:hrow + 64, 128 * qt:128 * (qt + 1)],
                                c["kt_"][hrow:hrow + 64, 512 * kb:512 * (kb + 1)],
                                True, True, reads=(t_qt, t_kt),
                                writes=(trks[h2],))
                    for h2 in range(2):
                        act(lambda e, pk=P[2 * sl + h2], h2=h2, kb=kb:
                            e.activation(
                                out=c["dum"][:], in_=pk[:, 0:512], func=AF.Exp,
                                scale=0.125,
                                accum_out=c["les"][:, 8 * h2 + kb:
                                                   8 * h2 + kb + 1]),
                            reads=(trks[h2],), writes=(c["t_dum"], c["t_les"]))
                    self.fill(bank=2 * SL[c["rot"] % 3])
                steps.append(s_emit)
            def c_emit(qt=qt):
                for h2 in range(2):
                    colc = 4 * h2 + (qt % 4)
                    dve(lambda e, h2=h2: e.tensor_reduce(
                        out=c["st1"][:, 8:9],
                        in_=c["les"][:, 8 * h2:8 * h2 + qb + 1],
                        axis=AX.X, op=ALU.add),
                        reads=(c["t_les"],), writes=(c["t_st1"],))
                    act(lambda e: e.activation(out=c["st1"][:, 8:9],
                        in_=c["st1"][:, 8:9], func=AF.Ln),
                        reads=(c["t_st1"],), writes=(c["t_st1"],))
                    dve(lambda e, colc=colc: e.tensor_scalar(
                        out=c["nc4"][:, colc:colc + 1], in0=c["st1"][:, 8:9],
                        scalar1=-8.0, scalar2=CSHIFT, op0=ALU.mult,
                        op1=ALU.add),
                        reads=(c["t_st1"],), writes=(c["t_nc4"],))
            steps.append(c_emit)
        for h2 in range(2):
            def f_emit(h2=h2):
                sl = SL[c["rot"] % 3]
                c["rot"] += 1
                pb, pbank = tP[2 * sl + h2], P[2 * sl + h2]
                nsl = 512 * (qb % 2)
                for qi in range(4):
                    self.mm(pbank[0:1, 128 * qi:128 * (qi + 1)],
                            c["nc4"][:, 4 * h2 + qi:4 * h2 + qi + 1],
                            self.identb[:], True, True,
                            reads=(c["t_nc4"], self.t_identb), writes=(pb,))
                dve(lambda e, pk=pbank, h2=h2, nsl=nsl: e.tensor_copy(
                    c["negc"][64 * h2:64 * h2 + 1, nsl:nsl + 512],
                    pk[0:1, 0:512]), reads=(pb,), writes=(c["t_negc"],))
                h2g = 2 * hp + h2
                dma("sync",
                    c["rpp"][32 * h2g + 6:32 * h2g + 7, 512 * qb:512 * (qb + 1)],
                    c["negc"][64 * h2:64 * h2 + 1, nsl:nsl + 512],
                    reads=(c["t_negc"],), writes=(c["t_rppB"][qb],))
            steps.append(f_emit)
        return steps

    def kloop(self, g, hp, qb, c, extra=()):
        pe, dve, act, dma, mm, tr = (self.pe, self.dve, self.act, self.dma,
                                     self.mm, self.tr)
        P, P2, tP = self.P, self.P2, self.tP
        qt_, kt_, vp_ = c["qt_"], c["kt_"], c["vp_"]
        t_qt, t_kt, t_vp = c["t_qt"], c["t_kt"], c["t_vp"]
        nkc = 4 * qb + 4
        extra = list(extra)
        nsl_rd = 512 * (qb % 2)
        ei = 0
        SL = (0, 1, 3)
        kc_slot = {}

        def logits_phase(kc):
            diag = kc >= 4 * qb
            sl = SL[c["rot"] % 3]
            c["rot"] += 1
            kc_slot[kc] = sl
            pl2 = P2[sl][:]
            trks = (tP[2 * sl], tP[2 * sl + 1])
            ptsl = c["ptb"][:, 1024 * (kc % 2):1024 * (kc % 2 + 1)]
            tpt = c["t_ptbS"][kc % 2]
            if not diag:
                for h2 in range(2):
                    h2g = 2 * hp + h2
                    self.mm(P[2 * sl + h2][:, 0:512],
                            c["jpp"][32 * h2g:32 * h2g + 7,
                                     1536 * (qb - 1) + 128 * kc:
                                     1536 * (qb - 1) + 128 * (kc + 1)],
                            c["rpp"][32 * h2g:32 * h2g + 7,
                                     512 * qb:512 * (qb + 1)],
                            True, False, reads=(c["t_jpp"], c["t_rppB"][qb]),
                            writes=(trks[h2],),
                            tp=(96, 0) if h2g == 3 else None)
                for h2 in range(2):
                    hrow = 64 * h2
                    self.mm(P[2 * sl + h2][:, 0:512],
                            kt_[hrow:hrow + 64, 128 * kc:128 * (kc + 1)],
                            qt_[hrow:hrow + 64, 512 * qb:512 * (qb + 1)],
                            False, True, reads=(t_kt, t_qt),
                            writes=(trks[h2],))
                act(lambda e, o=ptsl, pk=pl2: e.activation(
                    out=o, in_=pk, func=AF.Exp),
                    reads=trks, writes=(tpt,))
            else:
                cc = kc - 4 * qb
                sx = SL[c["rot"] % 3]
                c["rot"] += 1
                px2 = P2[sx][:]
                xtrks = (tP[2 * sx], tP[2 * sx + 1])
                for h2 in range(2):
                    h2g = 2 * hp + h2
                    self.mm(P[2 * sx + h2][:, 0:512],
                            c["jwb"][32 * h2g:32 * h2g + 6,
                                     128 * kc:128 * (kc + 1)],
                            c["rtg"][32 * h2g:32 * h2g + 6,
                                     512 * qb:512 * (qb + 1)],
                            True, True, reads=(c["t_jwb"], c["t_rtg"]),
                            writes=(xtrks[h2],),
                            tp=(96, 0) if h2g == 3 else None)
                dve(lambda e, pk=px2, cc=cc: e.tensor_tensor(
                    out=c["tdg"][:], in0=pk,
                    in1=c["wdg"][:, 1024 * cc:1024 * (cc + 1)],
                    op=ALU.mult),
                    reads=xtrks + (c["t_wdg"],), writes=(c["t_tdg"],))
                for h2 in range(2):
                    hrow = 64 * h2
                    self.mm(P[2 * sl + h2][:, 0:512],
                            kt_[hrow:hrow + 64, 128 * kc:128 * (kc + 1)],
                            qt_[hrow:hrow + 64, 512 * qb:512 * (qb + 1)],
                            True, False, reads=(t_kt, t_qt),
                            writes=(trks[h2],))
                for h2 in range(2):
                    self.mm(P[2 * sl + h2][:, 0:512],
                            self.onesb[64 * h2:64 * h2 + 1, 0:128],
                            c["negc"][64 * h2:64 * h2 + 1, nsl_rd:nsl_rd + 512],
                            False, True, reads=(self.t_onesb, c["t_negc"]),
                            writes=(trks[h2],))
                dve(lambda e, pk=pl2: e.tensor_tensor(
                    out=pk, in0=pk, in1=c["tdg"][:], op=ALU.add),
                    reads=trks + (c["t_tdg"],), writes=trks)
                act(lambda e, o=ptsl, pk=pl2: e.activation(
                    out=o, in_=pk, func=AF.Exp),
                    reads=trks, writes=(tpt,))
                dve(lambda e, o=ptsl, cc=cc: e.tensor_tensor(
                    out=o.rearrange("p (h f) -> p h f", h=2),
                    in0=o.rearrange("p (h f) -> p h f", h=2),
                    in1=self.m2c[:, 512 * cc:512 * (cc + 1)].rearrange(
                        "p (o f) -> p o f", o=1).broadcast_to([128, 2, 512]),
                    op=ALU.mult),
                    reads=(tpt, self.t_m2c), writes=(tpt,))
            fsl = SL[c["rot"] % 3]
            self.mm(P[2 * fsl][0:1, 0:512], self.onesb[0:1, 0:1],
                    self.onesb[0:1, 0:512], True, True,
                    reads=(self.t_onesb,), writes=(tP[2 * fsl],))

        def pv_phase(kc):
            for h2 in range(2):
                self.mm(P[4 + h2][0:65, 0:512],
                        vp_[:, 132 * kc:132 * kc + 65] if h2 == 0 else
                        vp_[:, 132 * kc + 66:132 * kc + 131],
                        c["ptb"][:, 1024 * (kc % 2) + 512 * h2:
                                 1024 * (kc % 2) + 512 * (h2 + 1)],
                        kc == 0, kc == nkc - 1,
                        reads=(t_vp, c["t_ptbS"][kc % 2]), writes=(tP[4 + h2],))

        for step in range(nkc + 1):
            if step < nkc:
                logits_phase(step)
            want = min(len(extra), ((step + 1) * len(extra)) // max(nkc - 1, 1))
            while ei < want:
                extra[ei]()
                ei += 1
            if step >= 1:
                pv_phase(step - 1)
        while ei < len(extra):
            extra[ei]()
            ei += 1

        dve(lambda e: e.tensor_copy(c["avu"][0:64, :], P2[2][:][0:64, :]),
            reads=(tP[4], tP[5]), writes=(c["t_avu"],))
        act(lambda e: e.activation(out=c["lin"][64:65, :], in_=P2[2][:][64:65, :],
            func=AF.Ln), reads=(tP[4], tP[5]), writes=(c["t_lin"],))
        act(lambda e: e.activation(out=c["lin"][64:65, :],
            in_=c["lin"][64:65, :], func=AF.Exp, scale=-1.0),
            reads=(c["t_lin"],), writes=(c["t_lin"],))
        for h2 in range(2):
            self.mm(P[4 + h2][0:64, 0:512], self.onesb[64:65, 0:64],
                    c["lin"][64:65, 512 * h2:512 * (h2 + 1)], True, True,
                    reads=(self.t_onesb, c["t_lin"]), writes=(tP[4 + h2],))
        for h2 in range(2):
            h = 4 * g + 2 * hp + h2
            ch = h // 2
            rr = 64 * (h % 2)
            dve(lambda e, h2=h2, ch=ch, rr=rr, qb=qb: e.tensor_tensor(
                out=self.avt[rr:rr + 64,
                             2048 * ch + 512 * qb:2048 * ch + 512 * (qb + 1)],
                in0=c["avu"][0:64, 512 * h2:512 * (h2 + 1)],
                in1=P[4 + h2][0:64, 0:512],
                op=ALU.mult),
                reads=(c["t_avu"], tP[4 + h2]), writes=(self.t_avt,))

    def wout(self, b):
        pe, dve, act, dma, mm, tr = (self.pe, self.dve, self.act, self.dma,
                                     self.mm, self.tr)
        P, tP, TBk, tTB = self.P, self.tP, self.TBk, self.tTB
        nc = self.nc
        dx = self.din["L0_attn_x"]

        (wsl0, wsl1, wsl2, wsl3, wldp, xtp) = self._wbufs
        wslabs = [wsl0, wsl1, wsl2, wsl3]
        wld2, t_wl2 = wldp
        xt2, t_xt2 = xtp
        for rc in range(8):
            dst = wld2[:, 1024 * (rc % 2):1024 * (rc % 2 + 1)]
            dma("pool", dst, self.din["L0_attn_w_out"][128 * rc:128 * (rc + 1), :],
                writes=(t_wl2,))
            for ci in range(8):
                tb = TBk[:, 128 * (ci % 8):128 * (ci % 8 + 1)]
                tr(tb, dst[:, 128 * ci:128 * (ci + 1)], self.identb[:],
                   reads=(t_wl2, self.t_identb), writes=(tTB,))
                wt, t_wt = wslabs[ci // 2]
                dve(lambda e, tb=tb, ci=ci, rc=rc, wt=wt: e.tensor_copy(
                    wt[:, 1024 * (ci % 2) + 128 * rc:
                       1024 * (ci % 2) + 128 * (rc + 1)],
                    tb), reads=(tTB,), writes=(t_wt,))
        for tt in range(NT):
            xsl = xt2[:, 1024 * (tt % 2):1024 * (tt % 2 + 1)]
            dma("sync", xsl, dx[b, 128 * tt:128 * (tt + 1), :], writes=(t_xt2,))
            for dh in range(2):
                pb, pbank = tP[(2 * tt + dh) % 2], P[(2 * tt + dh) % 2]
                self.mm(pbank[:, 0:512], self.onesb[0:1, 0:128],
                        self.borow[0:1, 512 * dh:512 * (dh + 1)], True, False,
                        reads=(self.t_onesb, self.t_borow), writes=(pb,))
                for ci in range(8):
                    wt, t_wt = wslabs[ci // 2]
                    self.mm(pbank[:, 0:512],
                            self.avt[:, 2048 * ci + 128 * tt:2048 * ci + 128 * (tt + 1)],
                            wt[:, 1024 * (ci % 2) + 512 * dh:
                               1024 * (ci % 2) + 512 * (dh + 1)],
                            False, ci == 7,
                            reads=(self.t_avt, t_wt), writes=(pb,))
                dve(lambda e, pk=pbank, xs=xsl, dh=dh: e.tensor_tensor(
                    out=xs[:, 512 * dh:512 * (dh + 1)],
                    in0=xs[:, 512 * dh:512 * (dh + 1)], in1=pk[:, 0:512],
                    op=ALU.add), reads=(pb, t_xt2), writes=(t_xt2,))
            dma("sync", self.dscr[b, 128 * tt:128 * (tt + 1), :], xsl,
                reads=(t_xt2,), writes=(self.t_dscr,))

    def ffn(self, b):
        pe, dve, act, dma, mm, tr = (self.pe, self.dve, self.act, self.dma,
                                     self.mm, self.tr)
        P, tP, TBk, tTB = self.P, self.tP, self.TBk, self.tTB
        nc = self.nc

        fx = contextlib.ExitStack()
        sbf = lambda n, s, d: self.sb(fx, f"{n}{b}", s, d)
        oac, t_oac = sbf("oac", [128, 16 * 1024], F32)
        xn2T, t_x2 = sbf("xn2T", [128, 8 * 2048], BF16)
        sel32, t_sel32 = sbf("sel32", [32, 32 * 128], BF16)
        selstg, t_sst = sbf("selstg", [32, 1024], BF16)
        selT, t_selT = sbf("selT", [128, 8 * 32], BF16)
        gates, t_g = sbf("gates", [128, 16 * 32], F32)
        gwork, t_gw = sbf("gwork", [128, 32], F32)
        gind, t_gi = sbf("gind", [128, 32], F32)
        gwb, t_gwb = sbf("gwb", [128, 32], BF16)
        gT, t_gT = sbf("gT", [32, 2048], BF16)
        gb, t_gb = sbf("gb", [128, 2 * 2048], BF16)
        spr, t_spr0 = sbf("spr", [128, 2 * 8192], BF16)
        t_spr = [t_spr0, Trk(self.bar)]
        kbf, t_kbf0 = sbf("kbf", [128, 2 * 4096], BF16)
        t_kbf = [t_kbf0, Trk(self.bar)]
        vbf, t_vbf0 = sbf("vbf", [128, 2 * 4096], BF16)
        t_vbf = [t_vbf0, Trk(self.bar)]
        xn2s, t_x2s = sbf("xn2s", [128, 1024], BF16)
        st2, t_st2 = sbf("st2", [128, 16], F32)

        dma("sync", sel32[:], self.din["c_sel32"], writes=(t_sel32,))
        dma("pool", selstg[:], self.din["L0_ffn_expert_sel"], writes=(t_sst,))
        for ci in range(8):
            tb = TBk[:, 128 * (ci % 8):128 * (ci % 8) + 32]
            tr(tb, selstg[:, 128 * ci:128 * (ci + 1)], self.identb[0:32, 0:32],
               reads=(t_sst, self.t_identb), writes=(tTB,))
            dve(lambda e, tb=tb, ci=ci: e.tensor_copy(
                selT[:, 32 * ci:32 * (ci + 1)], tb),
                reads=(tTB,), writes=(t_selT,))

        for tt in range(NT):
            osl = oac[:, 1024 * tt:1024 * (tt + 1)]
            dma("sync", osl, self.dscr[b, 128 * tt:128 * (tt + 1), :],
                reads=(self.t_dscr,), writes=(t_oac,))
            dve(lambda e, xs=osl: e.tensor_reduce(
                out=st2[:, 0:1], in_=xs, axis=AX.X, op=ALU.add),
                reads=(t_oac,), writes=(t_st2,))
            dve(lambda e: e.tensor_scalar(out=st2[:, 0:1], in0=st2[:, 0:1],
                scalar1=1.0 / D, scalar2=None, op0=ALU.mult),
                reads=(t_st2,), writes=(t_st2,))
            act(lambda e, xs=osl: e.activation(
                out=xn2s[:], in_=xs, func=AF.Square, accum_out=st2[:, 1:2]),
                reads=(t_oac,), writes=(t_x2s, t_st2))
            dve(lambda e: e.tensor_scalar(out=st2[:, 1:2], in0=st2[:, 1:2],
                scalar1=1.0 / D, scalar2=None, op0=ALU.mult),
                reads=(t_st2,), writes=(t_st2,))
            dve(lambda e: e.tensor_tensor(out=st2[:, 2:3], in0=st2[:, 0:1],
                in1=st2[:, 0:1], op=ALU.mult), reads=(t_st2,), writes=(t_st2,))
            dve(lambda e: e.tensor_tensor(out=st2[:, 1:2], in0=st2[:, 1:2],
                in1=st2[:, 2:3], op=ALU.subtract),
                reads=(t_st2,), writes=(t_st2,))
            dve(lambda e: e.tensor_scalar(out=st2[:, 1:2], in0=st2[:, 1:2],
                scalar1=1e-5, scalar2=None, op0=ALU.add),
                reads=(t_st2,), writes=(t_st2,))
            act(lambda e: e.activation(out=st2[:, 1:2], in_=st2[:, 1:2],
                func=AF.Ln), reads=(t_st2,), writes=(t_st2,))
            act(lambda e: e.activation(out=st2[:, 1:2], in_=st2[:, 1:2],
                func=AF.Exp, scale=-0.5), reads=(t_st2,), writes=(t_st2,))
            dve(lambda e, xs=osl: e.tensor_scalar(
                out=xn2s[:], in0=xs, scalar1=st2[:, 0:1], scalar2=st2[:, 1:2],
                op0=ALU.subtract, op1=ALU.mult),
                reads=(t_oac, t_st2), writes=(t_x2s,))
            for ci in range(8):
                tb = TBk[:, 128 * (ci % 8):128 * (ci % 8 + 1)]
                tr(tb, xn2s[:, 128 * ci:128 * (ci + 1)], self.identb[:],
                   reads=(t_x2s, self.t_identb), writes=(tTB,))
                dve(lambda e, tb=tb, ci=ci, tt=tt: e.tensor_copy(
                    xn2T[:, 2048 * ci + 128 * tt:2048 * ci + 128 * (tt + 1)],
                    tb), reads=(tTB,), writes=(t_x2,))
            pb, pbank = tP[4], P[4]
            for ci in range(8):
                self.mm(pbank[:, 0:32],
                        xn2T[:, 2048 * ci + 128 * tt:2048 * ci + 128 * (tt + 1)],
                        selT[:, 32 * ci:32 * (ci + 1)],
                        ci == 0, ci == 7, reads=(t_x2, t_selT), writes=(pb,))
            gsl = gates[:, 32 * tt:32 * (tt + 1)]
            act(lambda e, g_=gsl, pk=pbank: e.activation(
                out=g_, in_=pk[:, 0:32], func=AF.Sigmoid),
                reads=(pb,), writes=(t_g,))
            dve(lambda e, g_=gsl: e.tensor_copy(gwork[:], g_),
                reads=(t_g,), writes=(t_gw,))
            for rnd in range(4):
                dve(lambda e: e.tensor_reduce(out=st2[:, 3:4], in_=gwork[:],
                    axis=AX.X, op=ALU.max), reads=(t_gw,), writes=(t_st2,))
                if rnd < 3:
                    dve(lambda e: e.tensor_scalar(
                        out=gind[:], in0=gwork[:], scalar1=st2[:, 3:4],
                        scalar2=None, op0=ALU.is_ge),
                        reads=(t_gw, t_st2), writes=(t_gi,))
                    dve(lambda e: e.scalar_tensor_tensor(
                        out=gwork[:], in0=gind[:], scalar=-1e30, in1=gwork[:],
                        op0=ALU.mult, op1=ALU.add),
                        reads=(t_gi, t_gw), writes=(t_gw,))
            dve(lambda e, g_=gsl: e.tensor_scalar(
                out=gind[:], in0=g_, scalar1=st2[:, 3:4], scalar2=None,
                op0=ALU.is_ge), reads=(t_g, t_st2), writes=(t_gi,))
            dve(lambda e, g_=gsl: e.tensor_tensor(
                out=gwb[:], in0=gind[:], in1=g_, op=ALU.mult),
                reads=(t_gi, t_g), writes=(t_gwb,))
            tb = TBk[0:32, 0:128]
            tr(tb, gwb[:], self.identb[:],
               reads=(t_gwb, self.t_identb), writes=(tTB,))
            dve(lambda e, tb=tb, tt=tt: e.tensor_copy(
                gT[0:32, 128 * tt:128 * (tt + 1)], tb),
                reads=(tTB,), writes=(t_gT,))

        for ep in range(NE // 2):
            for e2 in range(2):
                e_ = 2 * ep + e2
                sl = e_ % 2
                dma("pool", kbf[:, 4096 * sl:4096 * (sl + 1)].rearrange(
                    "p (c f) -> p c f", c=8),
                    self.din["L0_ffn_keys"][e_].rearrange(
                        "(c p) f -> p c f", p=128),
                    writes=(t_kbf[sl],))
                dma("pool", vbf[:, 4096 * sl:4096 * (sl + 1)].rearrange(
                    "p (c f) -> p c f", c=4),
                    self.din["L0_ffn_values"][e_].rearrange(
                        "(c p) f -> p c f", p=128),
                    writes=(t_vbf[sl],))
                for seg in range(4):
                    pb, pbank = tP[5], P[5]
                    self.mm(pbank[:, 0:512],
                            sel32[0:32, 128 * e_:128 * (e_ + 1)],
                            gT[0:32, 512 * seg:512 * (seg + 1)], True, True,
                            reads=(t_sel32, t_gT), writes=(pb,))
                    act(lambda e, pk=pbank, sl=sl, seg=seg: e.activation(
                        out=gb[:, 2048 * sl + 512 * seg:2048 * sl + 512 * (seg + 1)],
                        in_=pk[:, 0:512], func=AF.Identity),
                        reads=(pb,), writes=(t_gb,))
                for fc in range(4):
                    for seg in range(4):
                        pb, pbank = tP[seg % 4], P[seg % 4]
                        for ci in range(8):
                            self.mm(pbank[:, 0:512],
                                    kbf[:, 4096 * sl + 512 * ci + 128 * fc:
                                        4096 * sl + 512 * ci + 128 * (fc + 1)],
                                    xn2T[:, 2048 * ci + 512 * seg:
                                         2048 * ci + 512 * (seg + 1)],
                                    ci == 0, ci == 7,
                                    reads=(t_kbf[sl], t_x2), writes=(pb,))
                        dve(lambda e, pk=pbank, sl=sl, fc=fc, seg=seg:
                            e.scalar_tensor_tensor(
                                out=spr[:, 8192 * sl + 2048 * fc + 512 * seg:
                                        8192 * sl + 2048 * fc + 512 * (seg + 1)],
                                in0=pk[:, 0:512], scalar=0.0,
                                in1=gb[:, 2048 * sl + 512 * seg:
                                       2048 * sl + 512 * (seg + 1)],
                                op0=ALU.max, op1=ALU.mult),
                            reads=(pb, t_gb), writes=(t_spr[sl],))
            for tt in range(NT):
                for dh in range(2):
                    pi = 4 + (2 * tt + dh) % 3
                    pb, pbank = tP[pi], P[pi]
                    first = True
                    for e2 in range(2):
                        for fc in range(4):
                            self.mm(pbank[:, 0:512],
                                    spr[:, 8192 * e2 + 2048 * fc + 128 * tt:
                                        8192 * e2 + 2048 * fc + 128 * (tt + 1)],
                                    vbf[:, 4096 * e2 + 1024 * fc + 512 * dh:
                                        4096 * e2 + 1024 * fc + 512 * (dh + 1)],
                                    first, e2 == 1 and fc == 3,
                                    reads=(t_spr[e2], t_vbf[e2]), writes=(pb,))
                            first = False
                    dve(lambda e, pk=pbank, tt=tt, dh=dh: e.tensor_tensor(
                        out=oac[:, 1024 * tt + 512 * dh:1024 * tt + 512 * (dh + 1)],
                        in0=oac[:, 1024 * tt + 512 * dh:1024 * tt + 512 * (dh + 1)],
                        in1=pk[:, 0:512], op=ALU.add),
                        reads=(pb, t_oac), writes=(t_oac,))
        dma("sync", self.dout[b].rearrange("(tt p) d -> p tt d", p=128),
            oac[:].rearrange("p (tt d) -> p tt d", tt=16),
            reads=(t_oac,), writes=(Trk(self.bar),))
        fx.close()


_CACHE = {}


def kernel(**inputs):
    consts = host_consts()
    mk = MK(stage="full")
    nc = mk.build()
    x = np.ascontiguousarray(inputs["L0_attn_x"])
    in_maps = []
    for cid in range(NCORES):
        m = {"L0_attn_x": x[2 * cid:2 * cid + 2]}
        for k, v in inputs.items():
            if k != "L0_attn_x":
                m[k] = np.ascontiguousarray(v)
        for k, v in consts.items():
            m["c_" + k] = v
        in_maps.append(m)
    res = run_bass_kernel_spmd(nc, in_maps, core_ids=list(range(NCORES)))
    out = np.concatenate([r["out"] for r in res.results], axis=0)
    return out.reshape(16, T, D)
